# version 20
# baseline (speedup 1.0000x reference)
import numpy as np
import ml_dtypes
import concourse.bass as bass
import concourse.mybir as mybir
from concourse.bass_utils import run_bass_kernel_spmd

F32 = mybir.dt.float32
BF16 = mybir.dt.bfloat16
U8 = mybir.dt.uint8
I32 = mybir.dt.int32
AF = mybir.ActivationFunctionType
ALU = mybir.AluOpType
AX = mybir.AxisListType

NEG = -1.0e30


def cfg_full():
    return _derive(dict(NC=8, PASSES=2, D=1024, SPC=1024, WIN=512, H=8, DH=64,
                        HID=2730,
                        LN_EPS=1e-5, RMS_EPS=1.1920929e-07, QK_SCALE=8.0))


def cfg_mini():
    return _derive(dict(NC=2, PASSES=2, D=256, SPC=256, WIN=128, H=2, DH=64,
                        HID=128,
                        LN_EPS=1e-5, RMS_EPS=1.1920929e-07, QK_SCALE=8.0))


def _derive(c):
    c = dict(c)
    c["NSEG"] = c["NC"] * c["PASSES"]
    c["DCH"] = c["D"] // 128
    c["TT"] = c["SPC"] // 128
    c["NW"] = c["SPC"] // c["WIN"]
    c["QSUB"] = c["WIN"] // 128
    c["KSUB"] = 2 * c["WIN"] // 128
    c["INNER"] = c["H"] * c["DH"]
    c["NPAIR"] = max(c["H"] // 2, 1)
    c["PACK2"] = c["H"] >= 2
    c["ECH"] = max(c["INNER"] // 128, 1)
    c["JS"] = (c["HID"] + 127) // 128
    c["LEXT"] = c["WIN"] + 8
    c["REXT"] = 8
    c["XE"] = c["LEXT"] + c["SPC"] + c["REXT"]
    c["KTT"] = c["TT"] + c["WIN"] // 128
    c["BW"] = c["WIN"] + 128
    c["TCW"] = min(512, c["SPC"])
    c["TC"] = c["SPC"] // c["TCW"]
    c["SEQ"] = c["NSEG"] * c["SPC"]
    c["VEC_N"] = 11
    return c


class Tok:
    __slots__ = ("w", "readers")

    def __init__(self):
        self.w = None
        self.readers = []


class Flow:
    def __init__(self):
        self.items = {k: [] for k in ("T", "V", "S", "G", "Q")}
        self.counts = {k: 0 for k in ("T", "V", "S", "G", "Q")}
        self.dmaq = {"Q": {"n": 0, "cum": [0, 0]}, "G": {"n": 0, "cum": [0, 0]}}
        self.segs = []

    def cc(self, fn, reads=(), writes=(), tag=""):
        deps = self._collect(reads, writes)
        ci = len(self.segs)
        tok = ("C%d" % ci, "c", 1)
        self.segs.append((self.items, dict(fn=fn, deps=deps, tag=tag, ci=ci)))
        self.items = {k: [] for k in ("T", "V", "S", "G", "Q")}
        return self._commit(tok, reads, writes)

    def _collect(self, reads, writes):
        deps = []
        for b in reads:
            if b.w is not None:
                deps.append(b.w)
        for b in writes:
            if b.w is not None:
                deps.append(b.w)
            deps.extend(b.readers)
        return deps

    def _commit(self, tok, reads, writes):
        for b in reads:
            b.readers.append(tok)
        for b in writes:
            b.w = tok
            b.readers = []
        return tok

    def op(self, stream, fn, reads=(), writes=(), tag="", inc=1):
        deps = self._collect(reads, writes)
        idx = self.counts[stream] + inc
        self.counts[stream] = idx
        tok = (stream, "c", idx)
        self.items[stream].append(dict(kind="op", fn=fn, deps=deps, tag=tag, inc=inc))
        return self._commit(tok, reads, writes)

    def dma(self, queue, transfers, reads=(), writes=(), tag=""):
        deps = self._collect(reads, writes)
        st = self.dmaq[queue]
        si = st["n"] % 2
        st["n"] += 1
        prev = st["cum"][si]
        st["cum"][si] = prev + 16 * len(transfers)
        tok = (queue, "d%d" % si, st["cum"][si])
        self.items[queue].append(dict(kind="dma", transfers=transfers, deps=deps,
                                      si=si, self_wait=prev, tag=tag))
        return self._commit(tok, reads, writes)

    def flatten_for_check(self):
        allitems = {k: [] for k in ("T", "V", "S", "G", "Q")}
        for items, ccit in self.segs + [(self.items, None)]:
            for k in ("T", "V", "S", "G", "Q"):
                allitems[k].extend(items[k])
            if ccit is not None:
                k = "C%d" % ccit["ci"]
                allitems[k] = [dict(kind="op", deps=ccit["deps"], inc=1,
                                    tag=ccit["tag"], fn=None)]
        return allitems

    def check_deadlock(self):
        allitems = self.flatten_for_check()
        pos = {k: 0 for k in allitems}
        done = {k: 0 for k in allitems}
        dcnt = {("Q", 0): 0, ("Q", 1): 0, ("G", 0): 0, ("G", 1): 0}
        total = sum(len(v) for v in allitems.values())
        ndone = 0
        while ndone < total:
            progressed = False
            for st, items in allitems.items():
                while pos[st] < len(items):
                    it = items[pos[st]]
                    ok = True
                    for (ds, dk, dv) in it["deps"]:
                        if dk == "c":
                            if done[ds] < dv:
                                ok = False
                                break
                        else:
                            if dcnt[(ds, int(dk[1]))] < dv:
                                ok = False
                                break
                    if not ok:
                        break
                    if it["kind"] == "op":
                        done[st] += it.get("inc", 1)
                    else:
                        dcnt[(st, it["si"])] += 16 * len(it["transfers"])
                    pos[st] += 1
                    ndone += 1
                    progressed = True
            if not progressed:
                print("DEADLOCK; stuck items:")
                for st, items in allitems.items():
                    if pos[st] < len(items):
                        it = items[pos[st]]
                        print("  %s[%d] tag=%s deps=%s" % (st, pos[st], it["tag"],
                                                           it["deps"][:6]))
                raise RuntimeError("static deadlock")

    def emit(self, nc):
        sems = {s: nc.alloc_semaphore("sm" + s) for s in ("T", "V", "S", "G", "Q")}
        for q in ("Q", "G"):
            for i in (0, 1):
                sems["%sd%d" % (q, i)] = nc.alloc_semaphore("sm%sd%d" % (q, i))
        for i in range(len(self.segs)):
            sems["C%d" % i] = nc.alloc_semaphore("smC%d" % i)
        eng = dict(T="tensor", V="vector", S="scalar", G="gpsimd", Q="sync")
        hwall = {s: {} for s in eng}

        def mkwait(e, stream):
            hw = hwall[stream]

            def wait(sem, val):
                if val > hw.get(sem.name, 0):
                    e.wait_ge(sem, val)
                    hw[sem.name] = val
            return wait

        def runner(stream, items):
            def run(e):
                wait = mkwait(e, stream)
                mysem = sems[stream]
                for it in items[stream]:
                    for dep in it["deps"]:
                        dst, dk, dv = dep
                        if dst == stream and dk == "c":
                            if stream not in ("T", "G"):
                                wait(mysem, dv)
                            continue
                        wait(sems[dst if dk == "c" else dst + dk], dv)
                    if it["kind"] == "op":
                        it["fn"](e).then_inc(mysem, it["inc"])
                    else:
                        dsem = sems[stream + "d%d" % it["si"]]
                        if it["self_wait"] > 0:
                            wait(dsem, it["self_wait"])
                        for t in it["transfers"]:
                            t(e).then_inc(dsem, 16)
            return run

        allsegs = self.segs + [(self.items, None)]
        for items, ccit in allsegs:
            with nc.Block() as block:
                for st, en in eng.items():
                    getattr(block, en)(runner(st, items))
            if ccit is not None:
                with nc.Block() as block:
                    def ccrun(e, ccit=ccit):
                        wait = mkwait(e, "G")
                        for dep in ccit["deps"]:
                            dst, dk, dv = dep
                            if dst == "G" and dk == "c":
                                wait(sems["G"], dv)
                                continue
                            wait(sems[dst if dk == "c" else dst + dk], dv)
                        ccit["fn"](e).then_inc(sems["C%d" % ccit["ci"]], 1)
                    block.gpsimd(ccrun)


class Region:

    def __init__(self, handle, nelem, dt):
        self.h = handle
        self.nelem = nelem
        self.dt = dt
        self.cur = 0

    def alloc(self, rows, cols, dt):
        assert dt == self.dt, "arena dtype mismatch"
        off = (self.cur + 255) // 256 * 256
        assert off + cols <= self.nelem, "region overflow %d+%d>%d" % (off, cols, self.nelem)
        self.cur = off + cols
        self.last_off = off
        return self.h[:rows, off:off + cols]

    def reset(self):
        self.cur = 0


def host_tables(c, seg):
    DH, WIN, SPC, TT, KTT = c["DH"], c["WIN"], c["SPC"], c["TT"], c["KTT"]
    n = c["SEQ"]
    T0 = seg * SPC
    d = np.arange(DH)
    dd = d % 32
    inv = 1.0 / (10000.0 ** (2.0 * dd / DH))
    base = (2.0 * dd + 0.4 * DH) / (1.4 * DH)
    sgn = np.where(d < 32, -1.0, 1.0)

    def tab(tg, for_q):
        freqs = tg[:, None].astype(np.float64) * inv[None, :]
        power = (tg.astype(np.float64) - n // 2) / float(WIN // 2)
        scale = base[None, :] ** power[:, None]
        if not for_q:
            scale = 1.0 / scale
        return (np.cos(freqs) * scale, np.sin(freqs) * scale * sgn[None, :])

    def lay(a, nt):
        return np.ascontiguousarray(
            a.reshape(nt, 128, DH).transpose(1, 0, 2).reshape(128, nt * DH)
        ).astype(np.float32)

    cq, sq = tab(np.arange(T0, T0 + SPC), True)
    ck, sk = tab(np.clip(np.arange(T0 - WIN, T0 + SPC), 0, None), False)
    return (np.concatenate([lay(cq, TT), lay(sq, TT)], axis=1),
            np.concatenate([lay(ck, KTT), lay(sk, KTT)], axis=1))


def host_pack_dvec(vs):
    D = vs[0].shape[0]
    DCH = D // 128
    n = len(vs)
    out = np.zeros((128, DCH * n), np.float32)
    for i, v in enumerate(vs):
        for ch in range(DCH):
            out[:, ch * n + i] = v[128 * ch:128 * (ch + 1)]
    return out


def host_pack_taps(w):
    D, K = w.shape
    DCH = D // 128
    out = np.zeros((128, DCH * K), np.float32)
    for ch in range(DCH):
        out[:, ch * K:(ch + 1) * K] = w[128 * ch:128 * (ch + 1), :]
    return out


def host_tile_w(w, js):
    L, D, J = w.shape
    DCH = D // 128
    JP = js * 128
    wp = np.zeros((L, D, JP), np.float32)
    wp[:, :, :J] = w
    t = wp.reshape(L, DCH, 128, js, 128).transpose(0, 3, 2, 1, 4)
    return np.ascontiguousarray(t.reshape(L, js, 128, DCH * 128))


def host_tile_rows(w, fs):
    L, F, E2 = w.shape
    FP = fs * 128
    wp = np.zeros((L, FP, E2), np.float32)
    wp[:, :F, :] = w
    return np.ascontiguousarray(wp.reshape(L, fs, 128, E2))


class MK:
    def __init__(self, c):
        self.c = c
        for k, v in c.items():
            setattr(self, k, v)
        self.XO = 1
        self.nc = bass.Bass("TRN2", target_bir_lowering=False)
        self.fl = Flow()
        self.pgn = 0
        self.pgn2 = 0
        self.pgnF = 0
        self._nrm_ctr = 0

    def T_(self):
        return Tok()

    def npg(self):
        i = self.pgn % 3
        self.pgn += 1
        return self.PG[i], self.t_pg[i]

    def npgF(self):
        i = self.pgnF % 4
        self.pgnF += 1
        return self.PSC[i], self.t_psc[i]

    def npg2(self):
        i = self.pgn2 % 2
        self.pgn2 += 1
        return self.PG[i], self.t_pg[i]

    def sv(self, ch, i):
        return self.smallv[:, ch * self.VEC_N + i: ch * self.VEC_N + i + 1]

    def run_mm_plan(self, plan):
        pieces = []
        for (lo, hi, mk) in plan:
            while lo < hi:
                nb = min(hi, (lo // 512 + 1) * 512)
                pieces.append((lo, nb, mk))
                lo = nb
        first_bank = {}
        last_bank = {}
        for i, (lo, hi, mk) in enumerate(pieces):
            b = lo // 512
            if b not in first_bank:
                first_bank[b] = i
            last_bank[b] = i
        for i, (lo, hi, mk) in enumerate(pieces):
            b = lo // 512
            mk(lo, hi, first_bank[b] == i, last_bank[b] == i)

    def switch_region(self, rtok, tag):
        self.fl.op("V", lambda e: e.engine_nop(), writes=[rtok], tag="sw_" + tag)

    def build(self):
        self.declare_dram()
        from contextlib import ExitStack
        with ExitStack() as ctx:
            self.ctx = ctx
            self.declare_sbuf()
            self.phase_consts()
            for p in range(self.PASSES):
                self.PI = p
                self.switch_region(self.t_rB, "lnB")
                self.switch_region(self.t_rF, "lnF")
                self.switch_region(self.t_rA, "lnA")
                self.phase_module_ln()
                self.switch_region(self.t_rF, "convF")
                self.switch_region(self.t_rB, "convB")
                self.phase_recurrent()
                self.phase_layer(0)
                self.phase_ag1_and_s()
                self.phase_layer(1)
                self.phase_final()
            self.fl.check_deadlock()
            self.fl.emit(self.nc)
        return self.nc

    def declare_dram(self):
        nc = self.nc
        D, XE, INNER, ECH, JS, DCH = self.D, self.XE, self.INNER, self.ECH, self.JS, self.DCH
        TT, KTT, DH, WIN, NC_, SPC = self.TT, self.KTT, self.DH, self.WIN, self.NC, self.SPC
        NP, NSEG = self.PASSES, self.NSEG
        self.xT = nc.dram_tensor("xT", [NP, D, XE], F32, kind="ExternalInput")
        self.wqkv_d = nc.dram_tensor("wqkv", [2, D, 3 * INNER], F32, kind="ExternalInput")
        self.wo_d = nc.dram_tensor("wo_t", [2, ECH, 128, DCH * 128], F32, kind="ExternalInput")
        self.w1a_d = nc.dram_tensor("w1a_t", [2, JS, 128, DCH * 128], F32, kind="ExternalInput")
        self.w1g_d = nc.dram_tensor("w1g_t", [2, JS, 128, DCH * 128], F32, kind="ExternalInput")
        self.w2_d = nc.dram_tensor("w2_t", [2, JS, 128, DCH * 128], F32, kind="ExternalInput")
        self.outw_d = nc.dram_tensor("outw_t", [DCH, 128, DCH * 128], F32, kind="ExternalInput")
        self.gatew_d = nc.dram_tensor("gatew", [128, DCH * 7], F32, kind="ExternalInput")
        self.candw_d = nc.dram_tensor("candw", [128, DCH * 7], F32, kind="ExternalInput")
        self.dvec_d = nc.dram_tensor("dvecs", [128, DCH * self.VEC_N], F32, kind="ExternalInput")
        self.b1_d = nc.dram_tensor("b1pack", [128, 4 * JS], F32, kind="ExternalInput")
        self.qks_d = nc.dram_tensor("qkscale", [1, 4 * DH], F32, kind="ExternalInput")
        self.rowv_d = nc.dram_tensor("rowvec", [1, 4 * D], F32, kind="ExternalInput")
        self.tabq_d = nc.dram_tensor("tabq", [NP, 128, 2 * TT * DH], F32, kind="ExternalInput")
        self.tabk_d = nc.dram_tensor("tabk", [NP, 128, 2 * KTT * DH], F32, kind="ExternalInput")
        self.edge_d = nc.dram_tensor("edge_neg", [1, WIN], F32, kind="ExternalInput")
        self.carry_d = nc.dram_tensor("carry", [1, NP * 2 * NSEG], F32, kind="ExternalInput")
        self.onehot_d = nc.dram_tensor("onehot", [1, NP * NSEG], F32, kind="ExternalInput")
        self.prevrk_d = nc.dram_tensor("prevrk", [1, NP], I32, kind="ExternalInput")
        self.bcol_d = nc.dram_tensor("bcol", [128, NP], F32, kind="ExternalInput")
        self.oT = nc.dram_tensor("oT", [D, NP * SPC], F32, kind="ExternalOutput")

        self.ag0_in = [nc.dram_tensor("ag0i%d" % l, [128, 2 * DCH], F32)
                       for l in range(2)]
        self.ag0_out = [nc.dram_tensor("ag0o%d" % l, [NSEG, 128, 2 * DCH], F32)
                        for l in range(2)]
        self.AG1W = DCH * (WIN + 1)
        self.ag1_in = nc.dram_tensor("ag1i", [128, self.AG1W], BF16)
        self.ag1_out = nc.dram_tensor("ag1o", [NSEG, 128, self.AG1W], BF16)
        self.hg_dram = nc.dram_tensor("hgd", [JS, 128, SPC], BF16)
        self.h_dram = nc.dram_tensor("hd", [DCH, 128, SPC], BF16)
        self.xn_dram = nc.dram_tensor("xnd", [DCH, 128, XE], BF16)
        self.dbg_dram = [nc.dram_tensor("dbg%d" % i, [NP, DCH, 128, SPC + 1], F32)
                         for i in range(4)]
        self.y_dram = nc.dram_tensor("yd", [DCH, 128, SPC], F32)

        i = np.arange(128)[:, None]
        j = np.arange(128)[None, :]
        tri_lo = np.where(j < i, NEG, 0.0).astype(np.float32)
        tri_hi = np.where(j > i, NEG, 0.0).astype(np.float32)
        bf = ml_dtypes.bfloat16
        self.i_idn = nc.inline_tensor(np.eye(128, dtype=bf), "c_idn")
        self.i_idn32 = nc.inline_tensor(np.eye(128, dtype=np.float32), "c_idn32")
        self.i_onescb = nc.inline_tensor(np.ones((128, 1), dtype=bf), "c_o1c")
        self.i_onesrb = nc.inline_tensor(np.ones((1, 128), dtype=bf), "c_o1r")
        self.i_onesw32 = nc.inline_tensor(np.ones((33, 128), dtype=np.float32), "c_ow32")
        self.i_ones512 = nc.inline_tensor(np.ones((1, 512), dtype=np.float32),
                                          "c_o512")
        self.i_negw32 = nc.inline_tensor(np.full((1, 512), -1.0, dtype=np.float32),
                                         "c_nw32")
        self.i_oneswb = nc.inline_tensor(np.ones((1, max(WIN, 512)), dtype=bf), "c_owb")
        self.i_tlo = nc.inline_tensor(tri_lo.astype(bf), "c_tlo")
        self.i_thi = nc.inline_tensor(tri_hi.astype(bf), "c_thi")
        self.i_tloT = nc.inline_tensor(np.ascontiguousarray(tri_lo.T).astype(bf), "c_tloT")
        self.i_thiT = nc.inline_tensor(np.ascontiguousarray(tri_hi.T).astype(bf), "c_thiT")
        self.i_iota = nc.inline_tensor(
            np.arange(1, SPC + 1, dtype=np.float32).reshape(1, SPC), "c_iota")
        cv = np.zeros((128, 7), np.float32)
        cv[:, 0] = self.LN_EPS
        cv[:, 1] = self.RMS_EPS
        cv[:, 2] = -0.5
        cv[:, 3] = 1.0 / self.D
        cv[:, 4] = 1e-24
        cv[:, 5] = -20.0
        cv[:, 6] = -1.0
        self.i_cval = nc.inline_tensor(cv, "c_cval")

    def declare_sbuf(self):
        nc, E = self.nc, self.ctx.enter_context
        DCH, SPC, WIN, INNER, ECH, JS = (self.DCH, self.SPC, self.WIN, self.INNER,
                                         self.ECH, self.JS)
        XE, KTT, TT, DH, NC_, BW, TCW = (self.XE, self.KTT, self.TT, self.DH,
                                         self.NC, self.BW, self.TCW)

        def sb(name, shape, dt):
            return E(nc.sbuf_tensor(name, shape, dt))

        self.x1 = [sb("x1_%d" % ch, [128, SPC + 1], F32) for ch in range(DCH)]
        self.ofm = [[sb("ofm%d_%d" % (s, e2), [128, WIN], BF16) for e2 in range(ECH)]
                    for s in range(1)]
        self.smallv = sb("smallv", [128, DCH * self.VEC_N], F32)
        self.gwsb = sb("gwsb", [128, DCH * 7], F32)
        self.cwsb = sb("cwsb", [128, DCH * 7], F32)
        self.b1sb = sb("b1sb", [128, 4 * JS], F32)
        self.qkssb = sb("qkssb", [1, 4 * DH], F32)
        self.qksrot = sb("qksrot", [1, 4 * DH], F32)
        self.idsb = sb("idsb", [128, 128], BF16)
        self.id32sb = sb("id32sb", [128, 128], F32)
        self.onescb = sb("onescb", [128, 1], BF16)
        self.onesrb = sb("onesrb", [1, 128], BF16)
        self.onesw32 = sb("onesw32", [33, 128], F32)
        self.ones512 = sb("ones512", [1, 512], F32)
        self.oneswb = sb("oneswb", [1, max(WIN, 512)], BF16)
        self.tloTsb = sb("tloTsb", [128, 128], BF16)
        self.thiTsb = sb("thiTsb", [128, 128], BF16)
        self.cvalsb = sb("cvalsb", [128, 7], F32)
        self.edgebf = sb("edgebf", [1, WIN], BF16)
        self.carrysb = sb("carrysb", [1, self.PASSES * 2 * self.NSEG], F32)
        self.onehotsb = sb("onehotsb", [1, self.PASSES * self.NSEG], F32)
        self.prevsb = sb("prevsb", [1, self.PASSES], I32)
        self.bcolb = sb("bcolb", [128, self.PASSES], BF16)
        self.wq_sb = [sb("wq_%d" % ch, [128, 3 * INNER], BF16) for ch in range(DCH)]
        self.wo_sb = [sb("wo_%d" % o, [128, DCH * 128], BF16) for o in range(ECH)]
        self.w1_sb = [sb("w1s_%d" % i, [128, DCH * 128], BF16) for i in range(2)]
        self.rstd = sb("rstd", [128, 4 * self.H], F32)
        self.bstage = sb("bstage", [128, DCH], F32)
        self.fstage = sb("fstage", [128, DCH], F32)
        self.bsel = sb("bsel", [128, DCH * self.NSEG], F32)
        self.fincol = sb("fincol", [128, self.NSEG * DCH], F32)
        self.cwt = sb("cwt", [128, self.NSEG], F32)
        self.hincol = sb("hincol", [128, DCH], F32)
        self.stg1 = sb("stg1", [128, TCW], F32)
        self.stg2 = sb("stg2", [128, TCW], F32)
        self.dgw = [sb("dgw%d" % t, [128, 128], BF16) for t in range(7)]
        self.dcw = [sb("dcw%d" % t, [128, 128], BF16) for t in range(7)]
        self.decayc = sb("decayc", [128, DCH], F32)
        rows1 = sb("rows1", [33, XE], F32)
        rows2 = sb("rows2", [1, XE], F32)
        rows3 = sb("rows3", [1, XE], F32)
        self.statr_sum = rows1[0:1, :]
        self.murow = rows1[0:1, :]
        self.rmsrow = rows1[32:33, 0:SPC]
        self.statr_sq = rows2[0:1, :]
        self.rsrow = rows2[0:1, :]
        self.mrrow = rows3[0:1, :]
        self.lmu = self.murow[:, 0:SPC]
        self.lrs = self.rsrow[:, 0:SPC]
        self.lmr = self.mrrow[:, 0:SPC]


        def tenant(sizes):
            cur = 0
            for s in sizes:
                cur = (cur + 255) // 256 * 256 + s
            return cur

        szA = max(tenant([XE] * DCH), tenant([SPC] * (DCH + 1)))
        regA = Region(sb("regA", [128, szA], BF16), szA, BF16)
        self.xn = [regA.alloc(128, XE, BF16) for _ in range(DCH)]
        regA.reset()
        self.xln = [regA.alloc(128, SPC, BF16) for _ in range(DCH)]
        self.glv = regA.alloc(128, SPC, BF16)
        regA.reset()
        self.x1b = [regA.alloc(128, XE, BF16) for _ in range(DCH)]

        self.JSH = (JS + 1) // 2
        nwt = WIN // 128
        szB = max(
            tenant([WIN] * (2 * ECH) + [INNER] * (2 * nwt) + [WIN] * ECH
                   + [INNER] * (2 * nwt) + [WIN] * 6
                   + [INNER] * 5 + [2 * TT * DH, 2 * KTT * DH]),
            tenant([XE]),
            tenant([SPC]),
            tenant([SPC] * 4 + [DCH * 128] * self.JSH + [TCW] * self.JSH
                   + [TCW] * 4 + [DCH * 128] * 4),
            tenant([SPC] * DCH + [DCH * 128] * DCH + [TCW] * 3))
        regB = Region(sb("regB", [128, szB], BF16), szB, BF16)
        self.kfm = [[regB.alloc(128, WIN, BF16) for _ in range(ECH)] for _ in range(2)]
        self.vtm = [[regB.alloc(128, INNER, BF16) for _ in range(WIN // 128)]
                    for _ in range(2)]
        self.qfm = [[regB.alloc(128, WIN, BF16) for _ in range(ECH)] for _ in range(1)]
        self.qtm = [regB.alloc(128, INNER, BF16) for _ in range(WIN // 128)]
        self.ktm = [regB.alloc(128, INNER, BF16) for _ in range(WIN // 128)]
        self.wt = [[regB.alloc(128, WIN, BF16) for _ in range(2)]
                   for _ in range(2)]
        self.rt1 = [regB.alloc(128, INNER, BF16) for _ in range(2)]
        self.zbb = [regB.alloc(128, WIN, BF16) for _ in range(2)]
        self.qn = []
        self.qn_off = []
        for _ in range(2):
            self.qn.append(regB.alloc(128, INNER, BF16))
            self.qn_off.append(regB.last_off)
        self.qn_h, self.szB = regB.h, szB
        self.effq = regB.alloc(128, 2 * TT * DH, BF16)
        self.effq_off = regB.last_off
        self.effk = regB.alloc(128, 2 * KTT * DH, BF16)
        self.effk_off = regB.last_off
        usedB = regB.cur
        regB.reset()
        self.sqtmp = regB.alloc(128, XE, BF16)
        usedB = max(usedB, regB.cur)
        regB.reset()
        self.hbf = regB.alloc(128, SPC, BF16)
        usedB = max(usedB, regB.cur)
        regB.reset()
        self.hgst = [[regB.alloc(128, SPC, BF16) for _ in range(2)]
                     for _ in range(2)]
        self.w2t = [regB.alloc(128, DCH * 128, BF16) for _ in range(self.JSH)]
        self.hgw = [regB.alloc(128, TCW, BF16) for _ in range(self.JSH)]
        self.ysq_f = [regB.alloc(128, TCW, BF16) for _ in range(2)]
        self.sqb2_f = [regB.alloc(128, TCW, BF16) for _ in range(2)]
        for _ in range(4):
            self.w1_sb.append(regB.alloc(128, DCH * 128, BF16))
        usedB = max(usedB, regB.cur)
        regB.reset()
        self.hres = [regB.alloc(128, SPC, BF16) for _ in range(DCH)]
        self.owf = [regB.alloc(128, DCH * 128, BF16) for _ in range(DCH)]
        self.xnre = [regB.alloc(128, TCW, BF16) for _ in range(2)]
        self.ysq_s = regB.alloc(128, TCW, BF16)
        usedB = max(usedB, regB.cur)
        assert usedB <= szB, (usedB, szB)

        CW = min(512, XE)
        szF = max(tenant([XE]), tenant([CW] * 2), tenant([SPC] * 3),
                  tenant([2 * WIN, INNER, INNER]),
                  tenant([TCW, SPC]))
        regF = Region(sb("regF", [128, szF], F32), szF, F32)
        self.xfp0 = regF.alloc(128, XE, F32)
        regF.reset()
        self.xfp = [regF.alloc(128, CW, F32), regF.alloc(128, CW, F32)]
        usedF = max(regF.cur, XE)
        regF.reset()
        self.gatef = regF.alloc(128, SPC, F32)
        self.candf = regF.alloc(128, SPC, F32)
        self.hstr = regF.alloc(128, SPC, F32)
        usedF = max(usedF, regF.cur)
        regF.reset()
        self.rowfl = regF.alloc(1, 2 * WIN, F32)
        self.qsq = [regF.alloc(128, INNER, F32) for _ in range(2)]
        usedF = max(usedF, regF.cur)
        regF.reset()
        self.powt = regF.alloc(128, TCW, F32)
        self.iotasb = regF.alloc(1, SPC, F32)
        usedF = max(usedF, regF.cur)
        regF.reset()
        self.yre = regF.alloc(128, SPC, F32)
        self.xre = regF.alloc(128, SPC, F32)
        self.obuf = regF.alloc(128, SPC, F32)
        usedF = max(usedF, regF.cur)
        assert usedF <= szF, usedF

        self.PG = [E(nc.psum_tensor("pg%d" % i, [128, 512], F32)) for i in range(3)]
        self.PSC = [E(nc.psum_tensor("psc%d" % i, [128, 512], F32))
                    for i in range(4)]
        self.PTB = E(nc.psum_tensor("ptb", [128, 1024], BF16))

        T = self.T_
        self.t_rA, self.t_rB = T(), T()
        self.t_x1 = [T() for _ in range(DCH)]
        self.t_ofm = [T() for _ in range(1)]
        self.t_const = T()
        self.t_small, self.t_gw, self.t_cw, self.t_b1 = T(), T(), T(), T()
        self.t_qks, self.t_qksr, self.t_rowv = T(), T(), T()
        self.t_wq = [T() for _ in range(DCH)]
        self.t_wo = [T() for _ in range(ECH)]
        self.t_w1 = [T() for _ in range(6)]
        self.t_eff = T()
        self.t_hbf = T()
        self.t_xnre = [T(), T()]
        self.t_rowfl = T()
        self.t_bstage, self.t_fstage, self.t_bsel = T(), T(), T()
        self.t_fincol, self.t_cwt, self.t_hincol = T(), T(), T()
        self.t_powt, self.t_stg1, self.t_stg2, self.t_ysq, self.t_sqb2 = \
            T(), T(), T(), T(), T()
        self.t_dgw = [T() for _ in range(7)]
        self.t_dcw = [T() for _ in range(7)]
        self.t_decay, self.t_rmsrow, self.t_statr = T(), T(), T()
        self.t_statq = T()
        
        self.t_xn = [T() for _ in range(DCH)]
        self.t_xln = [T() for _ in range(DCH)]
        self.t_gl = T()
        self.t_x1b = [T() for _ in range(DCH)]
        self.t_kfm = [T() for _ in range(3)]
        self.t_vtm = [T() for _ in range(3)]
        self.t_qfm = [T() for _ in range(1)]
        self.t_qtm, self.t_ktm = T(), T()
        self.t_wt = [[T(), T()] for _ in range(2)]
        self.t_qsq = [T(), T()]
        self.t_qn = [T(), T()]
        self.t_rt1 = [T(), T()]
        self.t_rstd = [T(), T()]
        self.t_zbb = [T(), T()]
        self.t_ysqf = [T(), T()]
        self.t_sqb2f = [T(), T()]
        self.t_xfp = [T()]
        self.t_xfp2 = [T(), T()]
        self.t_rF = T()
        self.t_rsrow, self.t_mrrow, self.t_sqtmp = T(), T(), T()
        self.t_gate, self.t_cand, self.t_hstr = T(), T(), T()
        self.t_hgst = [[T(), T()] for _ in range(2)]
        self.t_w2t = [T() for _ in range(JS)]
        self.t_hgw = [T() for _ in range(self.JSH)]
        self.t_hres = [T() for _ in range(DCH)]
        self.t_owf = [T() for _ in range(DCH)]
        self.t_yre, self.t_xre, self.t_obuf = T(), T(), T()
        self.t_pg = [T() for _ in range(3)]
        self.t_ptb = T()
        self.t_psc = [T() for _ in range(4)]
        self.t_ag0i = [T() for _ in range(2)]
        self.t_ag0o = [T() for _ in range(2)]
        self.t_ag1i, self.t_ag1o = T(), T()
        self.t_murow = self.t_statr
        self.t_hgd = [T() for _ in range(JS)]
        self.t_hd = [T() for _ in range(DCH)]
        self.t_xnd = [T() for _ in range(DCH)]
        self.t_yd = [T() for _ in range(DCH)]
        self.t_fin = T()
        self.t_pass, self.t_iota = T(), T()

    def phase_consts(self):
        fl = self.fl
        tr = []
        for dst, src in [
                (self.idsb, self.i_idn), (self.id32sb, self.i_idn32),
                (self.onescb, self.i_onescb),
                (self.onesrb, self.i_onesrb), (self.onesw32, self.i_onesw32),
                (self.ones512, self.i_ones512),
                (self.oneswb, self.i_oneswb), (self.tloTsb, self.i_tloT),
                (self.thiTsb, self.i_thiT),
                (self.cvalsb, self.i_cval),
                (self.prevsb, self.prevrk_d),
                (self.carrysb, self.carry_d), (self.onehotsb, self.onehot_d),
                (self.smallv, self.dvec_d), (self.gwsb, self.gatew_d),
                (self.cwsb, self.candw_d), (self.b1sb, self.b1_d),
                (self.qkssb, self.qks_d)]:
            tr.append(lambda e, dst=dst, src=src:
                      e.dma_start(out=dst[:, :], in_=src[:, :]))
        fl.dma("Q", tr, writes=[self.t_const, self.t_small, self.t_gw, self.t_cw,
                                self.t_b1, self.t_qks], tag="const")
        fl.dma("G", [
            lambda e: e.dma_start(out=self.edgebf[:, :], in_=self.edge_d[:, :]),
            lambda e: e.dma_start(out=self.bcolb[:, :], in_=self.bcol_d[:, :]),
        ], writes=[self.t_const], tag="constg")
        fl.op("S", lambda e: e.activation(
            self.decayc[:, :],
            self.smallv[:, :].rearrange("p (c n) -> p c n", n=self.VEC_N)[:, :, 2],
            AF.Exp), reads=[self.t_small], writes=[self.t_decay], tag="decay")

        def qksrot_fn(e):
            DH = self.DH
            src = bass.AP(self.qkssb, 32, [[4 * DH, 1], [DH, 4], [-32, 2], [1, 32]])
            return e.tensor_copy(
                self.qksrot[:, :].rearrange("p (r x i) -> p r x i", r=4, x=2), src)
        fl.op("V", qksrot_fn, reads=[self.t_qks], writes=[self.t_qksr], tag="qksrot")

    def phase_module_ln(self):
        fl, D, DCH, XE = self.fl, self.D, self.DCH, self.XE
        rB, rA = self.t_rB, self.t_rA
        nst = (XE + 511) // 512
        PI = self.PI
        for ch in range(DCH):
            fl.dma("Q", [lambda e, ch=ch, PI=PI: e.dma_start(
                out=self.xfp0, in_=self.xT[PI, 128 * ch:128 * (ch + 1), :])],
                reads=[self.t_rF, self.t_xfp[0]], writes=[self.t_xfp[0]], tag="ldx")
            fl.op("V", lambda e, ch=ch: e.tensor_copy(self.xn[ch], self.xfp0),
                  reads=[self.t_xfp[0], rA, self.t_rF], writes=[self.t_xn[ch]],
                  tag="castx")
        for t in range(nst):
            lo = t * 512
            w = min(XE, lo + 512) - lo
            p0, pt0 = self.PG[0], self.t_pg[0]
            p1, pt1 = self.PG[1], self.t_pg[1]
            for ch in range(DCH):
                fl.op("S", lambda e, ch=ch, lo=lo, w=w: e.square(
                    self.sqtmp[:, lo:lo + w], self.xn[ch][:, lo:lo + w]),
                    reads=[self.t_xn[ch], rB], writes=[self.t_sqtmp], tag="sq")
                fl.op("T", lambda e, ch=ch, lo=lo, w=w, p0=p0: e.matmul(
                    p0[0:1, 0:w], self.onescb[:, :], self.xn[ch][:, lo:lo + w],
                    start=(ch == 0), stop=(ch == DCH - 1)),
                    reads=[self.t_xn[ch], self.t_const], writes=[pt0], tag="stsum")
                fl.op("T", lambda e, ch=ch, lo=lo, w=w, p1=p1: e.matmul(
                    p1[0:1, 0:w], self.onescb[:, :], self.sqtmp[:, lo:lo + w],
                    start=(ch == 0), stop=(ch == DCH - 1)),
                    reads=[self.t_sqtmp, self.t_const], writes=[pt1], tag="stsq")
            fl.op("S", lambda e, lo=lo, w=w, p0=p0: e.copy(
                self.statr_sum[:, lo:lo + w], p0[0:1, 0:w]),
                reads=[pt0], writes=[self.t_statr], tag="strow0")
            fl.op("S", lambda e, lo=lo, w=w, p1=p1: e.copy(
                self.statr_sq[:, lo:lo + w], p1[0:1, 0:w]),
                reads=[pt1], writes=[self.t_rsrow], tag="strow1")
        fl.op("V", lambda e: e.tensor_scalar_mul(self.murow, self.statr_sum,
                                                 1.0 / D),
              reads=[rB], writes=[self.t_murow], tag="mu")
        fl.op("V", lambda e: e.tensor_mul(self.mrrow, self.murow, self.murow),
              reads=[self.t_murow, rB], writes=[self.t_mrrow], tag="mu2")
        fl.op("V", lambda e: e.scalar_tensor_tensor(
            self.rsrow, self.statr_sq, 1.0 / D, self.mrrow,
            ALU.mult, ALU.subtract),
            reads=[self.t_mrrow, rB], writes=[self.t_rsrow], tag="var")
        fl.op("S", lambda e: e.activation(self.rsrow, self.rsrow, AF.Ln,
                                          bias=self.cvalsb[0:1, 0:1]),
              reads=[rB], writes=[self.t_rsrow], tag="lnv")
        fl.op("S", lambda e: e.activation(self.rsrow, self.rsrow, AF.Exp, scale=self.cvalsb[0:1, 2:3]),
              reads=[rB], writes=[self.t_rsrow], tag="rstd")
        fl.op("V", lambda e: e.tensor_mul(self.mrrow, self.murow, self.rsrow),
              reads=[self.t_murow, self.t_rsrow, rB], writes=[self.t_mrrow], tag="murs")
        for t in range(nst):
            lo = t * 512
            w = min(XE, lo + 512) - lo
            pa, ta = self.npg()
            pb, tb = self.npg()
            fl.op("T", lambda e, lo=lo, w=w, pa=pa: e.matmul(
                pa[:, 0:w], self.onesw32[0:1, 0:128], self.rsrow[:, lo:lo + w],
                start=True, stop=True),
                reads=[self.t_rsrow, self.t_const, rB], writes=[ta], tag="rsB")
            fl.op("T", lambda e, lo=lo, w=w, pb=pb: e.matmul(
                pb[:, 0:w], self.onesw32[0:1, 0:128], self.mrrow[:, lo:lo + w],
                start=True, stop=True),
                reads=[self.t_mrrow, self.t_const, rB], writes=[tb], tag="mrB")
            for ch in range(DCH):
                sl = ch % 2
                fl.dma("Q", [lambda e, ch=ch, sl=sl, lo=lo, w=w, PI=PI:
                             e.dma_start(
                                 out=self.xfp[sl][:, 0:w],
                                 in_=self.xT[PI, 128 * ch:128 * (ch + 1),
                                             lo:lo + w])],
                    reads=[self.t_rF],
                    writes=[self.t_xfp2[sl], self.t_xfp[0]], tag="ldx2")
                fl.op("V", lambda e, sl=sl, w=w, pa=pa: e.tensor_mul(
                    self.xfp[sl][:, 0:w], self.xfp[sl][:, 0:w], pa[:, 0:w]),
                    reads=[ta, self.t_rF], writes=[self.t_xfp2[sl]], tag="xr")
                fl.op("V", lambda e, ch=ch, sl=sl, lo=lo, w=w, pb=pb: e.tensor_sub(
                    self.xn[ch][:, lo:lo + w], self.xfp[sl][:, 0:w], pb[:, 0:w]),
                    reads=[self.t_xfp2[sl], tb, rA, self.t_rF],
                    writes=[self.t_xn[ch]], tag="xnw")
        for ch in range(DCH):
            fl.dma("Q", [lambda e, ch=ch: e.dma_start(
                out=self.xn_dram[ch, :, :], in_=self.xn[ch])],
                reads=[self.t_xn[ch], rA], writes=[self.t_xnd[ch]], tag="xnsp")

    def phase_recurrent(self):
        fl, DCH, TC, TCW, LEXT, SPC = (self.fl, self.DCH, self.TC, self.TCW,
                                       self.LEXT, self.SPC)
        rB, rA = self.t_rF, self.t_rA
        for ch in range(DCH):
            for tp in range(7):
                fl.op("V", lambda e, ch=ch, tp=tp: e.tensor_scalar_mul(
                    self.dgw[tp][:, :], self.idsb[:, :],
                    self.gwsb[:, ch * 7 + tp:ch * 7 + tp + 1]),
                    reads=[self.t_const, self.t_gw], writes=[self.t_dgw[tp]], tag="dgw")
                fl.op("V", lambda e, ch=ch, tp=tp: e.tensor_scalar_mul(
                    self.dcw[tp][:, :], self.idsb[:, :],
                    self.cwsb[:, ch * 7 + tp:ch * 7 + tp + 1]),
                    reads=[self.t_const, self.t_cw], writes=[self.t_dcw[tp]], tag="dcw")
            for t in range(TC):
                base = LEXT + t * TCW
                pg_, tg_ = self.npg()
                pc_, tc_ = self.npg()
                for tp in range(7):
                    b2 = base + tp - 3
                    fl.op("T", lambda e, ch=ch, tp=tp, b2=b2, pg_=pg_: e.matmul(
                        pg_[:, 0:TCW], self.dgw[tp][:, :], self.xn[ch][:, b2:b2 + TCW],
                        start=(tp == 0), stop=(tp == 6)),
                        reads=[self.t_dgw[tp], self.t_xn[ch], rA],
                        writes=[tg_], tag="cvg")
                fl.op("S", lambda e, ch=ch, t=t, pg_=pg_: e.activation(
                    self.gatef[:, t * TCW:(t + 1) * TCW], pg_[:, 0:TCW], AF.Sigmoid,
                    bias=self.sv(ch, 0)),
                    reads=[tg_, self.t_small, rB], writes=[self.t_gate], tag="sig")
                for tp in range(7):
                    b2 = base + tp - 3
                    fl.op("T", lambda e, ch=ch, tp=tp, b2=b2, pc_=pc_: e.matmul(
                        pc_[:, 0:TCW], self.dcw[tp][:, :], self.xn[ch][:, b2:b2 + TCW],
                        start=(tp == 0), stop=(tp == 6)),
                        reads=[self.t_dcw[tp], self.t_xn[ch], rA],
                        writes=[tc_], tag="cvc")
                fl.op("S", lambda e, ch=ch, t=t, pc_=pc_: e.activation(
                    self.candf[:, t * TCW:(t + 1) * TCW], pc_[:, 0:TCW], AF.Tanh,
                    bias=self.sv(ch, 1)),
                    reads=[tc_, self.t_small, rB], writes=[self.t_cand], tag="tanh")
            fl.op("V", lambda e: e.tensor_mul(self.gatef, self.gatef, self.candf),
                  reads=[self.t_cand, rB], writes=[self.t_gate], tag="gated")
            fl.op("V", lambda e, ch=ch: e.tensor_tensor_scan(
                self.hstr[:, :], self.decayc[:, ch:ch + 1].broadcast_to([128, SPC]),
                self.gatef, 0.0, ALU.mult, ALU.add),
                reads=[self.t_gate, self.t_decay, rB], writes=[self.t_hstr], tag="scan")
            fl.op("V", lambda e, ch=ch: e.tensor_copy(
                self.fstage[:, ch:ch + 1], self.hstr[:, SPC - 1:SPC]),
                reads=[self.t_hstr, rB], writes=[self.t_fstage], tag="Fcol")
            fl.op("V", lambda e: e.tensor_copy(self.hbf[:, :], self.hstr[:, :]),
                  reads=[self.t_hstr, rB, self.t_rB], writes=[self.t_hbf], tag="hcast")
            fl.dma("Q", [lambda e, ch=ch: e.dma_start(
                out=self.h_dram[ch, :, :], in_=self.hbf[:, :])],
                reads=[self.t_hbf, self.t_rB], writes=[self.t_hd[ch]], tag="hsp")

    def x_src(self, l):
        if l == 0:
            return self.xn, self.t_xn
        return self.x1b, self.t_x1b

    def load_layer_weights(self, l):
        fl, DCH, ECH, DH, TT, KTT = self.fl, self.DCH, self.ECH, self.DH, self.TT, self.KTT
        fl.dma("G", [lambda e, ch=ch, l=l: e.dma_start(
            out=self.wq_sb[ch][:, :], in_=self.wqkv_d[l, 128 * ch:128 * (ch + 1), :])
            for ch in range(DCH)],
            writes=[self.t_wq[i] for i in range(DCH)], tag="wqL%d" % l)
        fl.dma("G", [lambda e, o=o, l=l: e.dma_start(
            out=self.wo_sb[o][:, :], in_=self.wo_d[l, o, :, :])
            for o in range(ECH)],
            writes=[self.t_wo[o] for o in range(ECH)], tag="woL%d" % l)
        PI, D = self.PI, self.D

    def emit_eff(self, l):
        fl, DH, TT, KTT, PI = self.fl, self.DH, self.TT, self.KTT, self.PI
        fl.dma("G", [
            lambda e, PI=PI: e.dma_start(out=self.effq[:, :], in_=self.tabq_d[PI, :, :]),
            lambda e, PI=PI: e.dma_start(out=self.effk[:, :], in_=self.tabk_d[PI, :, :]),
        ], reads=[self.t_rB], writes=[self.t_eff], tag="tabL%d" % l)
        for (dst, half, ntt, srow, rot, extra) in [
                (self.effq, 0, TT, 2 * l + 0, False, self.QK_SCALE),
                (self.effq, 1, TT, 2 * l + 0, True, self.QK_SCALE),
                (self.effk, 0, KTT, 2 * l + 1, False, 1.0),
                (self.effk, 1, KTT, 2 * l + 1, True, 1.0)]:
            ssb = self.qksrot if rot else self.qkssb
            stok = self.t_qksr if rot else self.t_qks
            col = half * ntt * DH
            pq, pqt = self.npg()
            fl.op("T", lambda e, pq=pq, ssb=ssb, srow=srow: e.matmul(
                pq[:, 0:DH], self.onesw32[0:1, 0:128],
                ssb[0:1, srow * DH:(srow + 1) * DH],
                start=True, stop=True),
                reads=[stok, self.t_const], writes=[pqt], tag="qsB")

            def f(e, dst=dst, col=col, ntt=ntt, pq=pq, extra=extra):
                b = bass.AP(pq, 0, [[512, 128], [0, ntt], [1, DH]])
                return e.scalar_tensor_tensor(
                    dst[:, col:col + ntt * DH].rearrange("p (t d) -> p t d", d=DH),
                    dst[:, col:col + ntt * DH].rearrange("p (t d) -> p t d", d=DH),
                    float(extra), b, ALU.mult, ALU.mult)
            fl.op("V", f, reads=[pqt, self.t_rB], writes=[self.t_eff], tag="eff")

    def emit_qkv_tile(self, l, kv_i, own_t):
        fl = self.fl
        DCH, INNER, WIN = self.DCH, self.INNER, self.WIN
        X, tX = self.x_src(l)
        ext = self.LEXT - WIN + 128 * kv_i
        wslot = (kv_i // (WIN // 128)) % 2
        tloc = kv_i % (WIN // 128)
        plan = ([("q", 0)] if own_t is not None else []) + \
            [("k", INNER), ("v", 2 * INNER)]
        for name, eoff in plan:
            p, pt = self.npg()
            for chd in range(DCH):
                sh = 1 if chd >= DCH // 2 else 0
                fl.op("T", lambda e, chd=chd, eoff=eoff, ext=ext, sh=sh, p=p:
                      e.matmul(p[:, 0:INNER],
                               X[chd][:, ext - sh:ext - sh + 128],
                               self.wq_sb[chd][:, eoff:eoff + INNER],
                               start=(chd == 0), stop=(chd == DCH - 1)),
                      reads=[tX[chd], self.t_wq[chd], self.t_rA], writes=[pt],
                      tag="qkv")
            if name == "v":
                fl.op("V", lambda e, p=p, wslot=wslot, tloc=tloc: e.tensor_copy(
                    self.vtm[wslot][tloc][:, :], p[:, 0:INNER]),
                    reads=[pt, self.t_rB], writes=[self.t_vtm[wslot]], tag="vev")
            else:
                self.emit_qk_norm_rot(name, p, pt, kv_i, own_t, tloc)

    def emit_qk_norm_rot(self, name, p, pt, kv_i, own_t, tloc):
        fl = self.fl
        H, DH, INNER, WIN = self.H, self.DH, self.INNER, self.WIN
        rB = self.t_rB
        is_q = name == "q"
        pr = self._nrm_ctr % 2
        self._nrm_ctr += 1
        ro = (0 if is_q else H) + 2 * H * pr
        qsq, tqsq = self.qsq[pr], self.t_qsq[pr]
        qn, tqn, qn_off = self.qn[pr], self.t_qn[pr], self.qn_off[pr]
        rt1, trt1 = self.rt1[pr], self.t_rt1[pr]
        trstd = self.t_rstd[pr]
        fl.op("S", lambda e, p=p, qsq=qsq: e.square(qsq[:, :], p[:, 0:INNER]),
              reads=[pt, rB, self.t_rF], writes=[tqsq], tag="l2sq")
        fl.op("V", lambda e, ro=ro, qsq=qsq: e.reduce_sum(
            self.rstd[:, ro:ro + H],
            qsq[:, :].rearrange("p (h d) -> p h d", d=DH), axis=AX.X),
            reads=[tqsq, rB, self.t_rF], writes=[trstd], tag="l2red")
        fl.op("S", lambda e, ro=ro: e.activation(
            self.rstd[:, ro:ro + H], self.rstd[:, ro:ro + H], AF.Ln,
            bias=self.cvalsb[:, 4:5]),
            reads=[self.t_const], writes=[trstd], tag="l2ln")
        fl.op("S", lambda e, ro=ro: e.activation(
            self.rstd[:, ro:ro + H], self.rstd[:, ro:ro + H], AF.Exp,
            scale=self.cvalsb[:, 2:3]),
            writes=[trstd], tag="l2ex")

        def qn_fn(e, p=p, ro=ro, qn=qn):
            b = bass.AP(self.rstd, ro, [[4 * H, 128], [1, H], [0, DH]])
            return e.tensor_tensor(
                qn[:, :].rearrange("p (h d) -> p h d", d=DH),
                p[:, 0:INNER].rearrange("p (h d) -> p h d", d=DH), b, ALU.mult)
        fl.op("V", qn_fn, reads=[pt, trstd, rB], writes=[tqn], tag="qn")

        eff_off = self.effq_off if is_q else self.effk_off
        ntt = self.TT if is_q else self.KTT
        ti = own_t if is_q else kv_i
        cs_off = ti * DH
        ss_off = ntt * DH + ti * DH

        def rot_fn(e, eff_off=eff_off, ss_off=ss_off, qn_off=qn_off, rt1=rt1):
            rotsrc = bass.AP(self.qn_h, qn_off + 32,
                             [[self.szB, 128], [DH, H], [-32, 2], [1, 32]])
            b = bass.AP(self.qn_h, eff_off + ss_off,
                        [[self.szB, 128], [0, H], [32, 2], [1, 32]])
            return e.tensor_tensor(
                rt1[:, :].rearrange("p (h x i) -> p h x i", x=2, i=32),
                rotsrc, b, ALU.mult)
        fl.op("V", rot_fn, reads=[tqn, self.t_eff, rB], writes=[trt1],
              tag="rot")

        def cs_fn(e, eff_off=eff_off, cs_off=cs_off, qn=qn):
            b = bass.AP(self.qn_h, eff_off + cs_off,
                        [[self.szB, 128], [0, H], [1, DH]])
            return e.tensor_tensor(
                qn[:, :].rearrange("p (h d) -> p h d", d=DH),
                qn[:, :].rearrange("p (h d) -> p h d", d=DH), b, ALU.mult)
        fl.op("V", cs_fn, reads=[self.t_eff, rB], writes=[tqn], tag="rcs")
        if is_q:
            dst, dtok = self.qtm[own_t % (WIN // 128)], self.t_qtm
        else:
            dst, dtok = self.ktm[tloc], self.t_ktm
        fl.op("V", lambda e, dst=dst, qn=qn, rt1=rt1: e.tensor_add(
            dst[:, :], qn[:, :], rt1[:, :]),
            reads=[tqn, trt1, rB], writes=[dtok], tag="radd")

    def emit_fm_transpose(self, which, wslot, qslot):
        fl, ECH, WIN = self.fl, self.ECH, self.WIN
        ntile = WIN // 128
        if which == "q":
            src, stok = self.qtm, self.t_qtm
            dsts, dtok = self.qfm[qslot], self.t_qfm[qslot]
        else:
            src, stok = self.ktm, self.t_ktm
            dsts, dtok = self.kfm[wslot], self.t_kfm[wslot]
        for eb in range(ECH):
            p, pt = self.PTB, self.t_ptb
            for tt in range(ntile):
                fl.op("T", lambda e, eb=eb, tt=tt, p=p: e.transpose(
                    p[:, tt * 128:(tt + 1) * 128],
                    src[tt][:, eb * 128:(eb + 1) * 128], self.idsb[:, :]),
                    reads=[stok, self.t_const, self.t_rB], writes=[pt], tag="tpfm")
            fl.op("V", lambda e, eb=eb, p=p: e.tensor_copy(
                dsts[eb][:, 0:WIN], p[:, 0:WIN]),
                reads=[pt, self.t_rB], writes=[dtok], tag="fmev")

    def emit_attention(self, w, qslot, slot_prev, slot_cur, first_glob):
        fl = self.fl
        QSUB, KSUB, WIN, DH, H = (self.QSUB, self.KSUB, self.WIN,
                                  self.DH, self.H)
        NPAIR, PACK2 = self.NPAIR, self.PACK2
        rB = self.t_rB
        nhp = 2 if PACK2 else 1
        kc_order = [QSUB - 1] + [k for k in range(KSUB) if k != QSUB - 1]
        for hp in range(NPAIR):
            pavs = [self.npg() for _ in range(nhp)]
            zps = [(self.PSC[2 + h01], self.t_psc[2 + h01])
                   for h01 in range(nhp)]
            for pos, kc in enumerate(kc_order):
                qlo = max(0, 128 * (kc - QSUB))
                qhi = min(WIN, 128 * kc + 128)
                wd = qhi - qlo
                sl = pos % 2
                for h01 in range(nhp):
                    hb = 64 * h01
                    wtb, wtok = self.wt[h01][sl], self.t_wt[h01][sl]
                    sci = (pos + h01) % 2
                    ps, pst = self.PSC[sci], self.t_psc[sci]
                    if kc < QSUB:
                        kt, ktok = self.kfm[slot_prev][hp], self.t_kfm[slot_prev]
                        kcol = kc * 128
                        mt, moff = self.tloTsb, wd - 128
                    else:
                        kt, ktok = self.kfm[slot_cur][hp], self.t_kfm[slot_cur]
                        kcol = (kc - QSUB) * 128
                        mt, moff = self.thiTsb, 0
                    fl.op("T", lambda e, kt=kt, kcol=kcol, hb=hb, hp=hp, qlo=qlo,
                          wd=wd, ps=ps: e.matmul(
                              ps[:, 0:wd], kt[hb:hb + 64, kcol:kcol + 128],
                              self.qfm[qslot][hp][hb:hb + 64, qlo:qlo + wd],
                              start=True, stop=False, tile_position=(hb, 0)),
                          reads=[ktok, self.t_qfm[qslot], rB], writes=[pst],
                          tag="sT")
                    if first_glob and kc < QSUB:
                        fl.op("T", lambda e, ps=ps, wd=wd: e.matmul(
                            ps[:, 0:wd], self.edgebf[:, 0:128],
                            self.oneswb[:, 0:wd],
                            start=False, stop=False),
                            reads=[self.t_const], writes=[pst], tag="eT")
                    fl.op("T", lambda e, ps=ps, mt=mt, moff=moff: e.matmul(
                        ps[:, moff:moff + 128], self.idsb[:, :], mt[:, :],
                        start=False, stop=True),
                        reads=[self.t_const], writes=[pst], tag="mT")
                    fl.op("S", lambda e, ps=ps, wtb=wtb, qlo=qlo, wd=wd:
                          e.activation(wtb[:, qlo:qlo + wd],
                                       ps[:, 0:wd], AF.Exp,
                                       bias=self.cvalsb[:, 5:6]),
                          reads=[pst, self.t_const, rB], writes=[wtok],
                          tag="wexp")
                for h01 in range(nhp):
                    hb = 64 * h01
                    h = 2 * hp + h01 if PACK2 else hp
                    wtb, wtok = self.wt[h01][sl], self.t_wt[h01][sl]
                    pav, pavt = pavs[h01]
                    zp, zpt = zps[h01]
                    if kc < QSUB:
                        vt, vtok = self.vtm[slot_prev][kc], self.t_vtm[slot_prev]
                    else:
                        vt, vtok = self.vtm[slot_cur][kc - QSUB], self.t_vtm[slot_cur]
                    fl.op("T", lambda e, vt=vt, h=h, hb=hb, wtb=wtb, pos=pos,
                          qlo=qlo, wd=wd, pav=pav: e.matmul(
                              pav[hb:hb + 64, qlo:qlo + wd],
                              vt[:, h * DH:(h + 1) * DH],
                              wtb[:, qlo:qlo + wd],
                              start=(pos == 0), stop=(pos == KSUB - 1),
                              tile_position=(0, hb)),
                          reads=[vtok, wtok, rB], writes=[pavt], tag="av")
                    fl.op("T", lambda e, wtb=wtb, pos=pos, qlo=qlo, wd=wd,
                          zp=zp: e.matmul(
                              zp[0:1, qlo:qlo + wd], self.onescb[:, :],
                              wtb[:, qlo:qlo + wd],
                              start=(pos == 0), stop=(pos == KSUB - 1)),
                          reads=[wtok, self.t_const, rB], writes=[zpt],
                          tag="zmm")
            for h01 in range(nhp):
                hb = 64 * h01
                pav, pavt = pavs[h01]
                zp, zpt = zps[h01]
                ps, pst = self.PSC[h01], self.t_psc[h01]
                rz = self.rowfl[0:1, h01 * WIN:(h01 + 1) * WIN]
                fl.op("S", lambda e, zp=zp, rz=rz: e.activation(
                    rz, zp[0:1, 0:WIN], AF.Ln),
                    reads=[zpt, self.t_rF], writes=[self.t_rowfl], tag="zln")
                fl.op("S", lambda e, rz=rz: e.activation(
                    rz, rz, AF.Exp, scale=self.cvalsb[0:1, 6:7]),
                    reads=[self.t_const, self.t_rF], writes=[self.t_rowfl],
                    tag="zrec")
                fl.op("T", lambda e, ps=ps, rz=rz: e.matmul(
                    ps[:, 0:WIN], self.onesw32[0:1, 0:128], rz,
                    start=True, stop=True),
                    reads=[self.t_rowfl, self.t_const, self.t_rF],
                    writes=[pst], tag="zB")
                fl.op("S", lambda e, ps=ps, h01=h01: e.copy(
                    self.zbb[h01][:, 0:WIN], ps[:, 0:WIN]),
                    reads=[pst, rB], writes=[self.t_zbb[h01]], tag="zev")
                fl.op("V", lambda e, pav=pav, hp=hp, hb=hb, h01=h01: e.tensor_mul(
                    self.ofm[0][hp][hb:hb + 64, 0:WIN],
                    pav[hb:hb + 64, 0:WIN], self.zbb[h01][hb:hb + 64, 0:WIN]),
                    reads=[pavt, self.t_zbb[h01], rB], writes=[self.t_ofm[0]],
                    tag="oev")

    def emit_wo(self, l, w):
        fl, DCH, ECH, WIN, XO = self.fl, self.DCH, self.ECH, self.WIN, self.XO
        for e2 in range(DCH):
            p, pt = self.npg()
            for o in range(ECH):
                fl.op("T", lambda e, e2=e2, o=o, w=w, p=p: e.matmul(
                    p[:, 0:WIN], self.wo_sb[o][:, e2 * 128:(e2 + 1) * 128],
                    self.ofm[0][o][:, 0:WIN],
                    start=(o == 0), stop=(o == ECH - 1)),
                    reads=[self.t_wo[o], self.t_ofm[0], self.t_rB],
                    writes=[pt], tag="womm")
            if l == 0:
                fl.op("V", lambda e, e2=e2, w=w, p=p: e.tensor_add(
                    self.x1[e2][:, XO + w * WIN:XO + (w + 1) * WIN],
                    self.xn[e2][:, self.LEXT + w * WIN:self.LEXT + (w + 1) * WIN],
                    p[:, 0:WIN]),
                    reads=[self.t_xn[e2], pt, self.t_rA],
                    writes=[self.t_x1[e2]], tag="xres")
            else:
                fl.op("V", lambda e, e2=e2, w=w, p=p: e.tensor_add(
                    self.x1[e2][:, XO + w * WIN:XO + (w + 1) * WIN],
                    self.x1[e2][:, XO + w * WIN:XO + (w + 1) * WIN],
                    p[:, 0:WIN]),
                    reads=[pt], writes=[self.t_x1[e2]], tag="xres")

    def emit_ag0(self, l):
        fl, DCH, NC_, SPC, XO = self.fl, self.DCH, self.NC, self.SPC, self.XO
        PI, NSEG = self.PI, self.NSEG
        nrk = NC_ * (PI + 1)
        for ch in range(DCH):
            fl.op("V", lambda e, ch=ch: e.tensor_copy(
                self.bstage[:, ch:ch + 1], self.x1[ch][:, XO + SPC - 1:XO + SPC]),
                reads=[self.t_x1[ch]], writes=[self.t_bstage], tag="bst")
        tr0 = [lambda e, l=l: e.dma_start(
            out=self.ag0_in[l][:, 0:DCH], in_=self.bstage[:, :]),
               lambda e, l=l: e.dma_start(
            out=self.ag0_in[l][:, DCH:2 * DCH], in_=self.fstage[:, :])]
        fl.dma("Q", tr0, reads=[self.t_bstage, self.t_fstage],
               writes=[self.t_ag0i[l]], tag="ag0i")
        fl.cc(lambda e, l=l, PI=PI: e.collective_compute(
            "AllGather", ALU.bypass, replica_groups=[list(range(NC_))],
            ins=[self.ag0_in[l].ap().opt()],
            outs=[self.ag0_out[l][PI * NC_:(PI + 1) * NC_, :, :].opt()]),
            reads=[self.t_ag0i[l]], writes=[self.t_ag0o[l]], tag="ag0")
        ag = self.ag0_out[l]
        fl.dma("Q", [lambda e, ag=ag, j=j: e.dma_start(
            out=self.bsel[:, j * DCH:(j + 1) * DCH], in_=ag[j, :, 0:DCH])
            for j in range(nrk)],
            reads=[self.t_ag0o[l]], writes=[self.t_bsel], tag="ag0rd")
        p, pt = self.npg()
        fl.op("T", lambda e, p=p, PI=PI, nrk=nrk: e.matmul(
            p[:, 0:nrk], self.onesw32[0:1, 0:128],
            self.onehotsb[0:1, PI * NSEG:PI * NSEG + nrk],
            start=True, stop=True),
            reads=[self.t_const], writes=[pt], tag="ohB")
        fl.op("V", lambda e, p=p, nrk=nrk: e.tensor_mul(
            self.bsel[:, 0:nrk * DCH].rearrange("p (j c) -> p j c", c=DCH),
            self.bsel[:, 0:nrk * DCH].rearrange("p (j c) -> p j c", c=DCH),
            bass.AP(p, 0, [[512, 128], [1, nrk], [0, DCH]])),
            reads=[pt], writes=[self.t_bsel], tag="ohM")
        fl.op("V", lambda e, nrk=nrk: e.reduce_sum(
            self.bstage[:, :],
            bass.AP(self.bsel, 0, [[NSEG * DCH, 128], [1, DCH], [DCH, nrk]]),
            axis=AX.X),
            reads=[self.t_bsel], writes=[self.t_bstage], tag="ohR")
        for ch in range(DCH):
            fl.op("V", lambda e, ch=ch: e.tensor_copy(
                self.x1[ch][:, 0:1], self.bstage[:, ch:ch + 1]),
                reads=[self.t_bstage], writes=[self.t_x1[ch]], tag="bwr")

    def emit_ffn(self, l):
        fl = self.fl
        DCH, SPC, TC, TCW, JS, D, XO = (self.DCH, self.SPC, self.TC, self.TCW,
                                        self.JS, self.D, self.XO)
        rB, rA = self.t_rB, self.t_rA

        def xsh(ch, lo, w):
            sh = 1 if ch >= DCH // 2 else 0
            return self.x1[ch][:, XO + lo - sh:XO + lo - sh + w]

        for t in range(TC):
            lo = t * TCW
            p0, pt0 = self.PG[0], self.t_pg[0]
            p1, pt1 = self.PG[1], self.t_pg[1]
            for ch in range(DCH):
                sl2 = ch % 2
                fl.op("V", lambda e, ch=ch, lo=lo, sl2=sl2: e.tensor_copy(
                    self.ysq_f[sl2][:, :], xsh(ch, lo, TCW)),
                    reads=[self.t_x1[ch], rB], writes=[self.t_ysqf[sl2]],
                    tag="lncst")
                fl.op("S", lambda e, sl2=sl2: e.square(
                    self.sqb2_f[sl2][:, :], self.ysq_f[sl2][:, :]),
                    reads=[self.t_ysqf[sl2], rB], writes=[self.t_sqb2f[sl2]],
                    tag="lnsq")
                fl.op("T", lambda e, p0=p0, ch=ch, sl2=sl2: e.matmul(
                    p0[0:1, 0:TCW], self.onescb[:, :], self.ysq_f[sl2][:, :],
                    start=(ch == 0), stop=(ch == DCH - 1)),
                    reads=[self.t_ysqf[sl2], self.t_const], writes=[pt0],
                    tag="lsum")
                fl.op("T", lambda e, p1=p1, ch=ch, sl2=sl2: e.matmul(
                    p1[0:1, 0:TCW], self.onescb[:, :], self.sqb2_f[sl2][:, :],
                    start=(ch == 0), stop=(ch == DCH - 1)),
                    reads=[self.t_sqb2f[sl2], self.t_const], writes=[pt1],
                    tag="lsq")
            fl.op("S", lambda e, lo=lo, p0=p0: e.copy(
                self.statr_sum[:, lo:lo + TCW], p0[0:1, 0:TCW]),
                reads=[pt0], writes=[self.t_statr], tag="lrow0")
            fl.op("S", lambda e, lo=lo, p1=p1: e.copy(
                self.statr_sq[:, lo:lo + TCW], p1[0:1, 0:TCW]),
                reads=[pt1], writes=[self.t_rsrow], tag="lrow1")
        fl.op("V", lambda e: e.tensor_scalar_mul(
            self.lmu[:, :], self.statr_sum[:, 0:SPC], 1.0 / D),
            reads=[rB], writes=[self.t_murow], tag="fmu")
        fl.op("V", lambda e: e.tensor_mul(self.lmr[:, :], self.lmu[:, :], self.lmu[:, :]),
              reads=[self.t_murow, rB], writes=[self.t_mrrow], tag="fmu2")
        fl.op("V", lambda e: e.scalar_tensor_tensor(
            self.lrs[:, :], self.statr_sq[:, 0:SPC], 1.0 / D, self.lmr[:, :],
            ALU.mult, ALU.subtract),
            reads=[self.t_mrrow, rB], writes=[self.t_rsrow], tag="fvar")
        fl.op("S", lambda e: e.activation(self.lrs[:, :], self.lrs[:, :], AF.Ln,
                                          bias=self.cvalsb[0:1, 0:1]),
              reads=[rB], writes=[self.t_rsrow], tag="flnv")
        fl.op("S", lambda e: e.activation(self.lrs[:, :], self.lrs[:, :], AF.Exp, scale=self.cvalsb[0:1, 2:3]),
              reads=[rB], writes=[self.t_rsrow], tag="frst")
        fl.op("V", lambda e: e.tensor_mul(self.lmr[:, :], self.lmu[:, :], self.lrs[:, :]),
              reads=[self.t_murow, self.t_rsrow, rB], writes=[self.t_mrrow], tag="fmurs")
        gcol = 5 if l == 0 else 8
        bcolv = 6 if l == 0 else 9
        for t in range(TC):
            lo = t * TCW
            pa, ta = self.npg()
            pb, tb = self.npg()
            fl.op("T", lambda e, lo=lo, pa=pa: e.matmul(
                pa[:, 0:TCW], self.onesw32[0:1, 0:128],
                self.lrs[0:1, lo:lo + TCW], start=True, stop=True),
                reads=[self.t_rsrow, self.t_const, rB], writes=[ta], tag="rgB")
            fl.op("T", lambda e, lo=lo, pb=pb: e.matmul(
                pb[:, 0:TCW], self.onesw32[0:1, 0:128],
                self.lmr[0:1, lo:lo + TCW], start=True, stop=True),
                reads=[self.t_mrrow, self.t_const, rB], writes=[tb], tag="cmb1")
            for ch in range(DCH):
                sg = self.stg1 if ch % 2 == 0 else self.stg2
                sgt = self.t_stg1 if ch % 2 == 0 else self.t_stg2
                fl.op("V", lambda e, ch=ch, lo=lo, pb=pb, sg=sg: e.tensor_sub(
                    sg[:, :], xsh(ch, lo, TCW), pb[:, 0:TCW]),
                    reads=[self.t_x1[ch], tb, rB], writes=[sgt], tag="xlr")
                fl.op("V", lambda e, pa=pa, sg=sg: e.tensor_mul(
                    sg[:, :], sg[:, :], pa[:, 0:TCW]),
                    reads=[ta, rB], writes=[sgt], tag="xlm")
                fl.op("V", lambda e, ch=ch, lo=lo, gcol=gcol, bcolv=bcolv, sg=sg:
                      e.tensor_scalar(
                          self.xln[ch][:, lo:lo + TCW], sg[:, :],
                          self.sv(ch, gcol), self.sv(ch, bcolv),
                          ALU.mult, ALU.add),
                      reads=[sgt, self.t_small, rA],
                      writes=[self.t_xln[ch]], tag="xlw")
        for ci in range(JS):
            sa, sg_ = (2 * ci) % 6, (2 * ci + 1) % 6
            fl.dma("G", [
                lambda e, ci=ci, sa=sa, l=l: e.dma_start(
                    out=self.w1_sb[sa][:, :], in_=self.w1a_d[l, ci, :, :]),
                lambda e, ci=ci, sg_=sg_, l=l: e.dma_start(
                    out=self.w1_sb[sg_][:, :], in_=self.w1g_d[l, ci, :, :]),
            ], reads=[rB], writes=[self.t_w1[sa], self.t_w1[sg_]], tag="w1ld")
            cb = ci % 2
            sa_, sg2 = (2 * ci) % 6, (2 * ci + 1) % 6
            ba = self.b1sb[:, (2 * l) * JS + ci:(2 * l) * JS + ci + 1]
            bg = self.b1sb[:, (2 * l + 1) * JS + ci:(2 * l + 1) * JS + ci + 1]
            hb_ = self.hgst[cb][0]
            hbt = self.t_hgst[cb][0]
            geb = self.hgst[cb][1]
            gebt = self.t_hgst[cb][1]
            for t in range(TC):
                lo = t * TCW
                pa_, pat = self.npgF()
                pg2, pgt = self.npgF()
                for d in range(DCH):
                    fl.op("T", lambda e, d=d, lo=lo, p=pa_, slot=sa_: e.matmul(
                        p[:, 0:TCW], self.w1_sb[slot][:, d * 128:(d + 1) * 128],
                        self.xln[d][:, lo:lo + TCW],
                        start=(d == 0), stop=(d == DCH - 1)),
                        reads=[self.t_w1[sa_], self.t_xln[d], rA, rB],
                        writes=[pat], tag="w1mm")
                for d in range(DCH):
                    fl.op("T", lambda e, d=d, lo=lo, p=pg2, slot=sg2: e.matmul(
                        p[:, 0:TCW], self.w1_sb[slot][:, d * 128:(d + 1) * 128],
                        self.xln[d][:, lo:lo + TCW],
                        start=(d == 0), stop=(d == DCH - 1)),
                        reads=[self.t_w1[sg2], self.t_xln[d], rA, rB],
                        writes=[pgt], tag="w1mm")
                fl.op("S", lambda e, p=pg2, lo=lo, geb=geb, bg=bg:
                      e.activation(geb[:, lo:lo + TCW], p[:, 0:TCW],
                                   AF.Gelu, bias=bg),
                      reads=[pgt, self.t_b1, rB], writes=[gebt], tag="gev")
                fl.op("V", lambda e, p=pa_, lo=lo, hb_=hb_, geb=geb, ba=ba:
                      e.scalar_tensor_tensor(
                          hb_[:, lo:lo + TCW], p[:, 0:TCW], ba,
                          geb[:, lo:lo + TCW], ALU.add, ALU.mult),
                      reads=[pat, gebt, self.t_b1, rB], writes=[hbt],
                      tag="hgstt")
            fl.dma("Q", [lambda e, ci=ci, hb_=hb_: e.dma_start(
                out=self.hg_dram[ci, :, :], in_=hb_[:, :])],
                reads=[hbt], writes=[self.t_hgd[ci]], tag="hgsp")
        JSH = self.JSH
        for fb in range(2):
            f0 = fb * JSH
            f1 = min(JS, f0 + JSH)
            if f0 >= f1:
                continue
            fl.dma("G", [lambda e, f=f, l=l, f0=f0: e.dma_start(
                out=self.w2t[f - f0][:, :], in_=self.w2_d[l, f, :, :])
                for f in range(f0, f1)],
                reads=[rB], writes=[self.t_w2t[f - f0] for f in range(f0, f1)],
                tag="w2ld")
            for t in range(TC):
                lo = t * TCW
                if (fb * TC + t) % 2 == 0:
                    hgb = self.hgw
                    hgbt = self.t_hgw
                else:
                    cps = max(1, (self.DCH * 128) // TCW)
                    hgb = [self.w1_sb[i // cps][:, (i % cps) * TCW:
                                                (i % cps + 1) * TCW]
                           for i in range(self.JSH)]
                    hgbt = [self.t_w1[i // cps] for i in range(self.JSH)]
                fl.dma("Q", [lambda e, f=f, lo=lo, f0=f0, hgb=hgb: e.dma_start(
                    out=hgb[f - f0][:, :], in_=self.hg_dram[f, :, lo:lo + TCW])
                    for f in range(f0, f1)],
                    reads=[self.t_hgd[f] for f in range(f0, f1)] + [rB],
                    writes=list({id(tk): tk for tk in
                                 [hgbt[f - f0] for f in range(f0, f1)]}.values()),
                    tag="hgld")
                for e2 in range(DCH):
                    p, pt = self.npgF()
                    for f in range(f0, f1):
                        fl.op("T", lambda e, f=f, e2=e2, p=p, f0=f0, f1=f1,
                              hgb=hgb: e.matmul(
                                  p[:, 0:TCW],
                                  self.w2t[f - f0][:, e2 * 128:(e2 + 1) * 128],
                                  hgb[f - f0][:, :],
                                  start=(f == f0), stop=(f == f1 - 1)),
                              reads=[self.t_w2t[f - f0], hgbt[f - f0], rB],
                              writes=[pt], tag="w2mm")
                    if fb == 0 and JS > JSH:
                        fl.op("V", lambda e, e2=e2, lo=lo, p=p: e.tensor_add(
                            self.x1[e2][:, XO + lo:XO + lo + TCW],
                            self.x1[e2][:, XO + lo:XO + lo + TCW], p[:, 0:TCW]),
                            reads=[pt], writes=[self.t_x1[e2]], tag="w2res0")
                    else:
                        b2c = self.sv(e2, 7 if l == 0 else 10)
                        fl.op("V", lambda e, e2=e2, lo=lo, p=p, b2c=b2c:
                              e.scalar_tensor_tensor(
                                  self.x1[e2][:, XO + lo:XO + lo + TCW],
                                  p[:, 0:TCW], b2c,
                                  self.x1[e2][:, XO + lo:XO + lo + TCW],
                                  ALU.add, ALU.add),
                              reads=[pt, self.t_small], writes=[self.t_x1[e2]],
                              tag="w2res")

    def phase_layer(self, l):
        fl, NW, QSUB = self.fl, self.NW, self.QSUB
        self.load_layer_weights(l)
        if l == 1:
            for ch in range(self.DCH):
                fl.op("V", lambda e, ch=ch: e.tensor_copy(
                    self.x1b[ch][:, self.LEXT:self.LEXT + self.SPC],
                    self.x1[ch][:, self.XO:self.XO + self.SPC]),
                    reads=[self.t_x1[ch], self.t_rA], writes=[self.t_x1b[ch]],
                    tag="x1bc")
        self.switch_region(self.t_rB, "attnL%d" % l)
        self.switch_region(self.t_rF, "attnF%d" % l)
        self.emit_eff(l)
        for w in range(NW):
            for j in range(QSUB):
                self.emit_qkv_tile(l, QSUB + w * QSUB + j, w * QSUB + j)
            self.emit_fm_transpose("k", (w + 1) % 2, 0)
            self.emit_fm_transpose("q", 0, 0)
            if w == 0:
                for j in range(QSUB):
                    self.emit_qkv_tile(l, j, None)
                self.emit_fm_transpose("k", 0, 0)
            self.emit_attention(w, 0, w % 2, (w + 1) % 2,
                                w == 0 and self.PI == 0)
            self.emit_wo(l, w)
        self.emit_ag0(l)
        self.emit_dbg(2 * l)
        self.switch_region(self.t_rB, "ffnL%d" % l)
        self.emit_ffn(l)
        self.emit_dbg(2 * l + 1)

    def emit_dbg(self, i):
        if not getattr(self, "DEBUG", False):
            return
        PI = self.PI
        for ch in range(self.DCH):
            self.fl.dma("G", [lambda e, ch=ch, i=i, PI=PI: e.dma_start(
                out=self.dbg_dram[i][PI, ch, :, :], in_=self.x1[ch][:, :])],
                reads=[self.t_x1[ch]], tag="dbg")

    def phase_ag1_and_s(self):
        fl = self.fl
        DCH, NC_, WIN, SPC, XO, TC, TCW = (self.DCH, self.NC, self.WIN, self.SPC,
                                           self.XO, self.TC, self.TCW)
        rB, rA = self.t_rB, self.t_rA
        AG1W = self.AG1W
        PI, NSEG = self.PI, self.NSEG
        nrk = NC_ * (PI + 1)
        co = PI * 2 * NSEG
        tr = []
        for ch in range(DCH):
            c0 = ch * (WIN + 1)
            tr.append(lambda e, ch=ch, c0=c0: e.dma_start(
                out=self.ag1_in[:, c0:c0 + WIN + 1],
                in_=self.x1[ch][:, XO + SPC - WIN - 1:XO + SPC]))
        fl.dma("G", tr, reads=[self.t_x1[ch] for ch in range(DCH)],
               writes=[self.t_ag1i], tag="ag1i")
        fl.cc(lambda e, PI=PI: e.collective_compute(
            "AllGather", ALU.bypass, replica_groups=[list(range(NC_))],
            ins=[self.ag1_in.ap().opt()],
            outs=[self.ag1_out[PI * NC_:(PI + 1) * NC_, :, :].opt()]),
            reads=[self.t_ag1i], writes=[self.t_ag1o], tag="ag1")

        self.rv_prev = [None]

        def ldprev(e, PI=PI):
            self.rv_prev[0] = e.value_load(self.prevsb[0:1, PI:PI + 1])
            return e.engine_nop()
        fl.op("G", ldprev, reads=[self.t_const], writes=[self.t_fin], tag="ldprev")

        tr = []
        for ch in range(DCH):
            c0 = ch * (WIN + 1)

            def t(e, ch=ch, c0=c0):
                src = self.ag1_out[bass.ds(self.rv_prev[0], 1), :, c0:c0 + WIN + 1]
                return e.dma_start(
                    out=self.x1b[ch][:, self.LEXT - WIN - 1:self.LEXT], in_=src)
            tr.append(t)
        fl.dma("G", tr, reads=[self.t_ag1o, self.t_fin, self.t_rA],
               writes=[self.t_x1b[ch] for ch in range(DCH)], tag="halo")
        for ch in range(DCH):
            fl.op("V", lambda e, ch=ch, PI=PI: e.tensor_tensor(
                self.x1b[ch][:, self.LEXT - 1:self.LEXT],
                self.x1b[ch][:, self.LEXT - 1:self.LEXT],
                self.bcolb[:, PI:PI + 1], ALU.mult),
                reads=[self.t_x1b[ch], self.t_const],
                writes=[self.t_x1b[ch]], tag="bgate")
        fl.dma("Q", [lambda e, j=j, PI=PI: e.dma_start(
            out=self.fincol[:, (PI * NC_ + j) * DCH:(PI * NC_ + j + 1) * DCH],
            in_=self.ag0_out[0][PI * NC_ + j, :, DCH:2 * DCH])
            for j in range(NC_)],
            reads=[self.t_ag0o[0]], writes=[self.t_fincol], tag="fread")

        self.switch_region(self.t_rB, "sasm")
        self.switch_region(self.t_rF, "sasmF")
        fl.dma("Q", [lambda e: e.dma_start(out=self.iotasb[:, :],
                                           in_=self.i_iota[:, :])],
               reads=[self.t_rF], writes=[self.t_iota], tag="iotald")
        fl.dma("Q", [lambda e, ch=ch: e.dma_start(
            out=self.hres[ch][:, :], in_=self.h_dram[ch, :, :]) for ch in range(DCH)],
            reads=[self.t_hd[ch] for ch in range(DCH)] + [rB],
            writes=[self.t_hres[ch] for ch in range(DCH)], tag="hre")


        pm, pmt = self.npg()
        fl.op("T", lambda e, pm=pm, co=co, nrk=nrk: e.matmul(
            pm[:, 0:nrk], self.onesw32[0:1, 0:128],
            self.carrysb[0:1, co + NSEG:co + NSEG + nrk],
            start=True, stop=True),
            reads=[self.t_const], writes=[pmt], tag="mgB")
        pp, ppt = self.npg()
        fl.op("T", lambda e, pp=pp, co=co, nrk=nrk: e.matmul(
            pp[:, 0:nrk], self.onesw32[0:1, 0:128],
            self.carrysb[0:1, co:co + nrk],
            start=True, stop=True),
            reads=[self.t_const], writes=[ppt], tag="cpB")
        for ch in range(DCH):
            fl.op("S", lambda e, ch=ch, pp=pp, nrk=nrk: e.activation(
                self.cwt[:, 0:nrk], pp[:, 0:nrk], AF.Exp, scale=self.sv(ch, 2)),
                reads=[ppt, self.t_small], writes=[self.t_cwt], tag="cwexp")
            fl.op("V", lambda e, pm=pm, nrk=nrk: e.tensor_mul(
                self.cwt[:, 0:nrk], self.cwt[:, 0:nrk], pm[:, 0:nrk]),
                reads=[pmt], writes=[self.t_cwt], tag="cwm")
            fl.op("V", lambda e, ch=ch, nrk=nrk: e.tensor_mul(
                self.cwt[:, 0:nrk], self.cwt[:, 0:nrk],
                bass.AP(self.fincol, ch, [[NSEG * DCH, 128], [DCH, nrk]])),
                reads=[self.t_fincol], writes=[self.t_cwt], tag="cwf")
            fl.op("V", lambda e, ch=ch, nrk=nrk: e.reduce_sum(
                self.hincol[:, ch:ch + 1], self.cwt[:, 0:nrk], axis=AX.X),
                reads=[self.t_cwt], writes=[self.t_hincol], tag="hin")
        for t in range(TC):
            lo = t * TCW
            pw, pwt = self.npg()
            fl.op("T", lambda e, lo=lo, pw=pw: e.matmul(
                pw[:, 0:TCW], self.onesw32[0:1, 0:128],
                self.iotasb[:, lo:lo + TCW], start=True, stop=True),
                reads=[self.t_const, self.t_iota, self.t_rF],
                writes=[pwt], tag="iotB")
            for ch in range(DCH):
                fl.op("S", lambda e, ch=ch, pw=pw: e.activation(
                    self.powt[:, :], pw[:, 0:TCW], AF.Exp, scale=self.sv(ch, 2)),
                    reads=[pwt, self.t_small, self.t_rF],
                    writes=[self.t_powt], tag="powt")
                fl.op("V", lambda e, ch=ch, lo=lo: e.scalar_tensor_tensor(
                    self.hres[ch][:, lo:lo + TCW], self.powt[:, :],
                    self.hincol[:, ch:ch + 1], self.hres[ch][:, lo:lo + TCW],
                    ALU.mult, ALU.add),
                    reads=[self.t_powt, self.t_hincol, rB, self.t_rF],
                    writes=[self.t_hres[ch]], tag="hcorr")
        fl.dma("G", [lambda e, ch=ch: e.dma_start(
            out=self.owf[ch][:, :], in_=self.outw_d[ch, :, :]) for ch in range(DCH)],
            reads=[rB], writes=[self.t_owf[ch] for ch in range(DCH)], tag="owld")
        for t in range(TC):
            lo = t * TCW
            prms, prmst = self.PG[2], self.t_pg[2]
            for e2 in range(DCH):
                p, pt = self.npg2()
                for ch in range(DCH):
                    fl.op("T", lambda e, ch=ch, e2=e2, lo=lo, p=p: e.matmul(
                        p[:, 0:TCW], self.owf[ch][:, e2 * 128:(e2 + 1) * 128],
                        self.hres[ch][:, lo:lo + TCW],
                        start=(ch == 0), stop=(ch == DCH - 1)),
                        reads=[self.t_owf[ch], self.t_hres[ch], rB],
                        writes=[pt], tag="opmm")
                xsl = e2 % 2
                fl.dma("Q", [lambda e, e2=e2, lo=lo, xsl=xsl: e.dma_start(
                    out=self.xnre[xsl][:, :],
                    in_=self.xn_dram[e2, :, self.LEXT + lo:self.LEXT + lo + TCW])],
                    reads=[self.t_xnd[e2], rB], writes=[self.t_xnre[xsl]],
                    tag="xnre")
                sg = self.stg1 if e2 % 2 == 0 else self.stg2
                sgt = self.t_stg1 if e2 % 2 == 0 else self.t_stg2
                fl.op("V", lambda e, p=p, e2=e2, sg=sg, xsl=xsl:
                      e.scalar_tensor_tensor(
                          sg[:, :], p[:, 0:TCW], self.sv(e2, 3),
                          self.xnre[xsl][:, :], ALU.add, ALU.add),
                      reads=[pt, self.t_small, self.t_xnre[xsl]],
                      writes=[sgt], tag="yev")
                fl.op("S", lambda e, sg=sg: e.square(self.ysq_s[:, :], sg[:, :]),
                      reads=[sgt, rB], writes=[self.t_ysq], tag="ysq")
                fl.op("T", lambda e, e2=e2, prms=prms: e.matmul(
                    prms[0:1, 0:TCW], self.onescb[:, :], self.ysq_s[:, :],
                    start=(e2 == 0), stop=(e2 == DCH - 1)),
                    reads=[self.t_ysq, self.t_const], writes=[prmst], tag="rmsmm")
                fl.dma("Q", [lambda e, e2=e2, lo=lo, sg=sg: e.dma_start(
                    out=self.y_dram[e2, :, lo:lo + TCW], in_=sg[:, :])],
                    reads=[sgt], writes=[self.t_yd[e2]], tag="ysp")
            fl.op("S", lambda e, lo=lo, prms=prms: e.copy(
                self.statr_sum[:, lo:lo + TCW], prms[0:1, 0:TCW]),
                reads=[prmst], writes=[self.t_statr], tag="rmsrow")
        fl.op("S", lambda e: e.activation(
            self.rmsrow[:, :], self.statr_sum[:, 0:SPC], AF.Ln,
            scale=self.cvalsb[0:1, 3:4], bias=self.cvalsb[0:1, 1:2]),
            reads=[self.t_statr], writes=[self.t_rmsrow], tag="rln")
        fl.op("S", lambda e: e.activation(self.rmsrow[:, :], self.rmsrow[:, :], AF.Exp,
                                          scale=self.cvalsb[0:1, 2:3]),
              writes=[self.t_rmsrow], tag="rex")

    def phase_final(self):
        fl = self.fl
        DCH, SPC, TC, TCW, XO = self.DCH, self.SPC, self.TC, self.TCW, self.XO
        PI = self.PI
        rB = self.t_rF
        self.switch_region(rB, "final")
        outtoks = []
        for ch in range(DCH):
            fl.dma("Q", [
                lambda e, ch=ch: e.dma_start(out=self.yre[:, :],
                                             in_=self.y_dram[ch, :, :]),
                lambda e, ch=ch, PI=PI: e.dma_start(
                    out=self.xre[:, :],
                    in_=self.xT[PI, 128 * ch:128 * (ch + 1),
                                self.LEXT:self.LEXT + SPC]),
            ], reads=[self.t_yd[ch], rB], writes=[self.t_yre, self.t_xre],
                tag="finld")
            for t in range(TC):
                lo = t * TCW
                fl.op("S", lambda e, ch=ch, lo=lo: e.activation(
                    self.stg1[:, :], self.x1[ch][:, XO + lo:XO + lo + TCW],
                    AF.Silu),
                    reads=[self.t_x1[ch]], writes=[self.t_stg1], tag="silu")
                pr, prt = self.npg()
                fl.op("T", lambda e, lo=lo, pr=pr: e.matmul(
                    pr[:, 0:TCW], self.onesw32[32:33, 0:128],
                    self.rmsrow[0:1, lo:lo + TCW], start=True, stop=True),
                    reads=[self.t_rmsrow, self.t_const], writes=[prt], tag="rmsB")
                fl.op("V", lambda e, ch=ch, lo=lo, pr=pr: e.scalar_tensor_tensor(
                    self.stg2[:, :], self.yre[:, lo:lo + TCW], self.sv(ch, 4),
                    pr[:, 0:TCW], ALU.mult, ALU.mult),
                    reads=[self.t_yre, prt, self.t_small, rB],
                    writes=[self.t_stg2], tag="smul")
                fl.op("V", lambda e: e.tensor_mul(
                    self.stg1[:, :], self.stg1[:, :], self.stg2[:, :]),
                    reads=[self.t_stg2], writes=[self.t_stg1], tag="sg")
                fl.op("V", lambda e, lo=lo: e.tensor_add(
                    self.obuf[:, lo:lo + TCW], self.stg1[:, :],
                    self.xre[:, lo:lo + TCW]),
                    reads=[self.t_stg1, self.t_xre, rB], writes=[self.t_obuf],
                    tag="oadd")
            tok = fl.dma("Q", [lambda e, ch=ch, PI=PI: e.dma_start(
                out=self.oT[128 * ch:128 * (ch + 1), PI * SPC:(PI + 1) * SPC],
                in_=self.obuf[:, :])],
                reads=[self.t_obuf, rB], tag="ost")
            outtoks.append(tok)
        fin = Tok()
        for tk in outtoks:
            fin.readers.append(tk) if False else None
        class _W:
            pass
        dummy = Tok()
        dummy.w = outtoks[-1]
        dummy.readers = list(outtoks[:-1])
        fl.op("V", lambda e: e.engine_nop(), writes=[dummy], tag="finwait")


def _prep_core_inputs(c, inputs, core):
    D, SPC, WIN, LEXT, REXT, XE = (c["D"], c["SPC"], c["WIN"], c["LEXT"],
                                   c["REXT"], c["XE"])
    NC_, JS, DCH, DH = c["NC"], c["JS"], c["DCH"], c["DH"]
    NP, NSEG = c["PASSES"], c["NSEG"]
    xfull = inputs["xTfull"]
    xe = np.zeros((NP, D, XE), np.float32)
    tabq_l, tabk_l = [], []
    edge = np.zeros((1, WIN), np.float32)
    carry = np.zeros((1, NP * 2 * NSEG), np.float32)
    onehot = np.zeros((1, NP * NSEG), np.float32)
    prevrk = np.zeros((1, NP), np.int32)
    bcol = np.ones((128, NP), np.float32)
    for p in range(NP):
        seg = p * NC_ + core
        T0 = seg * SPC
        lo = T0 - LEXT
        hi = T0 + SPC + REXT
        slo, shi = max(0, lo), min(c["SEQ"], hi)
        xe[p, :, slo - lo:shi - lo] = xfull[:, slo:shi]
        tq, tk = host_tables(c, seg)
        tabq_l.append(tq)
        tabk_l.append(tk)
        if seg == 0:
            edge[0, :] = NEG
            bcol[:, p] = 0.0
        co = p * 2 * NSEG
        for j in range(NSEG):
            if j < seg:
                carry[0, co + j] = SPC * (seg - 1 - j)
                carry[0, co + NSEG + j] = 1.0
        if seg > 0:
            onehot[0, p * NSEG + seg - 1] = 1.0
        prevrk[0, p] = (seg - 1) if seg > 0 else 0
    d = dict(inputs["shared"])
    d.update(dict(xT=xe, tabq=np.stack(tabq_l), tabk=np.stack(tabk_l),
                  edge_neg=edge, carry=carry,
                  onehot=onehot, prevrk=prevrk, bcol=bcol))
    return d


def prep_shared(c, L):
    D, JS, DCH, DH, NC_ = c["D"], c["JS"], c["DCH"], c["DH"], c["NC"]
    HID = c["HID"]
    wqkv = np.ascontiguousarray(L["ffn_wqkv"], np.float32)
    wo = host_tile_rows(np.ascontiguousarray(L["ffn_wo"], np.float32),
                        c["INNER"] // 128)
    w1 = np.ascontiguousarray(L["ffn_w1"], np.float32)
    w1a = host_tile_w(w1[:, :, :HID], JS)
    w1g = host_tile_w(w1[:, :, HID:], JS)
    w2 = host_tile_rows(np.ascontiguousarray(L["ffn_w2"], np.float32), JS)
    outw_t = np.ascontiguousarray(L["attn_out_w"], np.float32).T
    outw = np.ascontiguousarray(
        outw_t.reshape(DCH, 128, D)).astype(np.float32)
    gatew = host_pack_taps(L["attn_gate_w"].reshape(D, -1).astype(np.float32))
    candw = host_pack_taps(L["attn_cand_w"].reshape(D, -1).astype(np.float32))
    z = np.zeros(D, np.float32)
    dv = host_pack_dvec([
        L["attn_gate_b"].astype(np.float32), L["attn_cand_b"].astype(np.float32),
        L["attn_log_decay"].astype(np.float32), L["attn_out_b"].astype(np.float32),
        L["attn_norm_w"].astype(np.float32),
        L["ffn_ln_g"][0].astype(np.float32), L["ffn_ln_b"][0].astype(np.float32),
        L["ffn_b2"][0].astype(np.float32),
        L["ffn_ln_g"][1].astype(np.float32), L["ffn_ln_b"][1].astype(np.float32),
        L["ffn_b2"][1].astype(np.float32)])
    b1 = np.asarray(L["ffn_b1"], np.float32)
    JP = JS * 128
    b1p = np.zeros((128, 4 * JS), np.float32)
    for l in range(2):
        for grp in range(2):
            seg = b1[l, :HID] if grp == 0 else b1[l, HID:]
            segp = np.zeros(JP, np.float32)
            segp[:HID] = seg
            for ci in range(JS):
                b1p[:, (2 * l + grp) * JS + ci] = segp[128 * ci:128 * (ci + 1)]
    qks = np.concatenate([
        L["ffn_q_scale"][0:1], L["ffn_k_scale"][0:1],
        L["ffn_q_scale"][1:2], L["ffn_k_scale"][1:2]],
        axis=0).astype(np.float32).reshape(1, 4 * DH)
    rowv = np.stack([L["ffn_ln_g"][0], L["ffn_ln_b"][0],
                     L["ffn_ln_g"][1], L["ffn_ln_b"][1]]
                    ).astype(np.float32).reshape(1, 4 * D)
    return dict(wqkv=wqkv, wo_t=wo, w1a_t=w1a, w1g_t=w1g, w2_t=w2,
                outw_t=outw, gatew=gatew, candw=candw, dvecs=dv, b1pack=b1p,
                qkscale=qks, rowvec=rowv)


_NC_CACHE = {}


def _get_nc(key, c):
    if key not in _NC_CACHE:
        _NC_CACHE[key] = MK(c).build()
    return _NC_CACHE[key]


def run(c, inputs_by_name):
    L = {k[3:]: v for k, v in inputs_by_name.items() if k.startswith("L0_")}
    x = np.asarray(L["attn_x"], np.float32)[0]
    xTfull = np.ascontiguousarray(x.T)
    shared = prep_shared(c, L)
    nc = _get_nc(("mk", c["NC"], c["D"], c["SPC"]), c)
    in_maps = []
    base = dict(shared=shared, xTfull=xTfull)
    for core in range(c["NC"]):
        in_maps.append(_prep_core_inputs(c, base, core))
    res = run_bass_kernel_spmd(nc, in_maps, core_ids=list(range(c["NC"])))
    outs = [np.asarray(r["oT"], np.float32) for r in res.results]
    D, SPC, NC_, NP = c["D"], c["SPC"], c["NC"], c["PASSES"]
    full = np.empty((D, c["SEQ"]), np.float32)
    for core in range(NC_):
        for p in range(NP):
            seg = p * NC_ + core
            full[:, seg * SPC:(seg + 1) * SPC] = outs[core][:, p * SPC:(p + 1) * SPC]
    return np.ascontiguousarray(full.T)[None]


def kernel(**inputs):
    c = cfg_full()
    return run(c, inputs)


# version 22 (final)
# speedup vs baseline: 1.1263x; 1.1263x over previous
import numpy as np
import ml_dtypes
import concourse.bass as bass
import concourse.mybir as mybir
from concourse.bass_utils import run_bass_kernel_spmd

F32 = mybir.dt.float32
BF16 = mybir.dt.bfloat16
U8 = mybir.dt.uint8
I32 = mybir.dt.int32
AF = mybir.ActivationFunctionType
ALU = mybir.AluOpType
AX = mybir.AxisListType

NEG = -1.0e30


def cfg_full():
    return _derive(dict(NC=8, PASSES=2, D=1024, SPC=1024, WIN=512, H=8, DH=64,
                        HID=2730,
                        LN_EPS=1e-5, RMS_EPS=1.1920929e-07, QK_SCALE=8.0))


def cfg_mini():
    return _derive(dict(NC=2, PASSES=2, D=256, SPC=256, WIN=128, H=2, DH=64,
                        HID=128,
                        LN_EPS=1e-5, RMS_EPS=1.1920929e-07, QK_SCALE=8.0))


def _derive(c):
    c = dict(c)
    c["NSEG"] = c["NC"] * c["PASSES"]
    c["DCH"] = c["D"] // 128
    c["TT"] = c["SPC"] // 128
    c["NW"] = c["SPC"] // c["WIN"]
    c["QSUB"] = c["WIN"] // 128
    c["KSUB"] = 2 * c["WIN"] // 128
    c["INNER"] = c["H"] * c["DH"]
    c["NPAIR"] = max(c["H"] // 2, 1)
    c["PACK2"] = c["H"] >= 2
    c["ECH"] = max(c["INNER"] // 128, 1)
    c["JS"] = (c["HID"] + 127) // 128
    c["LEXT"] = c["WIN"] + 8
    c["REXT"] = 8
    c["XE"] = c["LEXT"] + c["SPC"] + c["REXT"]
    c["KTT"] = c["TT"] + c["WIN"] // 128
    c["BW"] = c["WIN"] + 128
    c["TCW"] = min(512, c["SPC"])
    c["TC"] = c["SPC"] // c["TCW"]
    c["SEQ"] = c["NSEG"] * c["SPC"]
    c["VEC_N"] = 11
    return c


class Tok:
    __slots__ = ("w", "readers")

    def __init__(self):
        self.w = None
        self.readers = []


class Flow:
    def __init__(self):
        self.items = {k: [] for k in ("T", "V", "S", "G", "Q")}
        self.counts = {k: 0 for k in ("T", "V", "S", "G", "Q")}
        self.dmaq = {"Q": {"n": 0, "cum": [0, 0]}, "G": {"n": 0, "cum": [0, 0]}}
        self.segs = []

    def cc(self, fn, reads=(), writes=(), tag=""):
        deps = self._collect(reads, writes)
        ci = len(self.segs)
        tok = ("C%d" % ci, "c", 1)
        self.segs.append((self.items, dict(fn=fn, deps=deps, tag=tag, ci=ci)))
        self.items = {k: [] for k in ("T", "V", "S", "G", "Q")}
        return self._commit(tok, reads, writes)

    def _collect(self, reads, writes):
        deps = []
        for b in reads:
            if b.w is not None:
                deps.append(b.w)
        for b in writes:
            if b.w is not None:
                deps.append(b.w)
            deps.extend(b.readers)
        return deps

    def _commit(self, tok, reads, writes):
        for b in reads:
            b.readers.append(tok)
        for b in writes:
            b.w = tok
            b.readers = []
        return tok

    def op(self, stream, fn, reads=(), writes=(), tag="", inc=1):
        deps = self._collect(reads, writes)
        idx = self.counts[stream] + inc
        self.counts[stream] = idx
        tok = (stream, "c", idx)
        self.items[stream].append(dict(kind="op", fn=fn, deps=deps, tag=tag, inc=inc))
        return self._commit(tok, reads, writes)

    def dma(self, queue, transfers, reads=(), writes=(), tag=""):
        deps = self._collect(reads, writes)
        st = self.dmaq[queue]
        si = st["n"] % 2
        st["n"] += 1
        prev = st["cum"][si]
        st["cum"][si] = prev + 16 * len(transfers)
        tok = (queue, "d%d" % si, st["cum"][si])
        self.items[queue].append(dict(kind="dma", transfers=transfers, deps=deps,
                                      si=si, self_wait=prev, tag=tag))
        return self._commit(tok, reads, writes)

    def flatten_for_check(self):
        allitems = {k: [] for k in ("T", "V", "S", "G", "Q")}
        for items, ccit in self.segs + [(self.items, None)]:
            for k in ("T", "V", "S", "G", "Q"):
                allitems[k].extend(items[k])
            if ccit is not None:
                k = "C%d" % ccit["ci"]
                allitems[k] = [dict(kind="op", deps=ccit["deps"], inc=1,
                                    tag=ccit["tag"], fn=None)]
        return allitems

    def check_deadlock(self):
        allitems = self.flatten_for_check()
        pos = {k: 0 for k in allitems}
        done = {k: 0 for k in allitems}
        dcnt = {("Q", 0): 0, ("Q", 1): 0, ("G", 0): 0, ("G", 1): 0}
        total = sum(len(v) for v in allitems.values())
        ndone = 0
        while ndone < total:
            progressed = False
            for st, items in allitems.items():
                while pos[st] < len(items):
                    it = items[pos[st]]
                    ok = True
                    for (ds, dk, dv) in it["deps"]:
                        if dk == "c":
                            if done[ds] < dv:
                                ok = False
                                break
                        else:
                            if dcnt[(ds, int(dk[1]))] < dv:
                                ok = False
                                break
                    if not ok:
                        break
                    if it["kind"] == "op":
                        done[st] += it.get("inc", 1)
                    else:
                        dcnt[(st, it["si"])] += 16 * len(it["transfers"])
                    pos[st] += 1
                    ndone += 1
                    progressed = True
            if not progressed:
                print("DEADLOCK; stuck items:")
                for st, items in allitems.items():
                    if pos[st] < len(items):
                        it = items[pos[st]]
                        print("  %s[%d] tag=%s deps=%s" % (st, pos[st], it["tag"],
                                                           it["deps"][:6]))
                raise RuntimeError("static deadlock")

    def emit(self, nc):
        sems = {s: nc.alloc_semaphore("sm" + s) for s in ("T", "V", "S", "G", "Q")}
        for q in ("Q", "G"):
            for i in (0, 1):
                sems["%sd%d" % (q, i)] = nc.alloc_semaphore("sm%sd%d" % (q, i))
        for i in range(len(self.segs)):
            sems["C%d" % i] = nc.alloc_semaphore("smC%d" % i)
        eng = dict(T="tensor", V="vector", S="scalar", G="gpsimd", Q="sync")
        hwall = {s: {} for s in eng}

        def mkwait(e, stream):
            hw = hwall[stream]

            def wait(sem, val):
                if val > hw.get(sem.name, 0):
                    e.wait_ge(sem, val)
                    hw[sem.name] = val
            return wait

        def runner(stream, items):
            def run(e):
                wait = mkwait(e, stream)
                mysem = sems[stream]
                for it in items[stream]:
                    for dep in it["deps"]:
                        dst, dk, dv = dep
                        if dst == stream and dk == "c":
                            if stream not in ("T", "G"):
                                wait(mysem, dv)
                            continue
                        wait(sems[dst if dk == "c" else dst + dk], dv)
                    if it["kind"] == "op":
                        it["fn"](e).then_inc(mysem, it["inc"])
                    else:
                        dsem = sems[stream + "d%d" % it["si"]]
                        if it["self_wait"] > 0:
                            wait(dsem, it["self_wait"])
                        for t in it["transfers"]:
                            t(e).then_inc(dsem, 16)
            return run

        allsegs = self.segs + [(self.items, None)]
        for items, ccit in allsegs:
            with nc.Block() as block:
                for st, en in eng.items():
                    getattr(block, en)(runner(st, items))
            if ccit is not None:
                with nc.Block() as block:
                    def ccrun(e, ccit=ccit):
                        wait = mkwait(e, "G")
                        for dep in ccit["deps"]:
                            dst, dk, dv = dep
                            if dst == "G" and dk == "c":
                                wait(sems["G"], dv)
                                continue
                            wait(sems[dst if dk == "c" else dst + dk], dv)
                        ccit["fn"](e).then_inc(sems["C%d" % ccit["ci"]], 1)
                    block.gpsimd(ccrun)


class Region:

    def __init__(self, handle, nelem, dt):
        self.h = handle
        self.nelem = nelem
        self.dt = dt
        self.cur = 0

    def alloc(self, rows, cols, dt):
        assert dt == self.dt, "arena dtype mismatch"
        off = (self.cur + 255) // 256 * 256
        assert off + cols <= self.nelem, "region overflow %d+%d>%d" % (off, cols, self.nelem)
        self.cur = off + cols
        self.last_off = off
        return self.h[:rows, off:off + cols]

    def reset(self):
        self.cur = 0


def host_tables(c, seg):
    DH, WIN, SPC, TT, KTT = c["DH"], c["WIN"], c["SPC"], c["TT"], c["KTT"]
    n = c["SEQ"]
    T0 = seg * SPC
    d = np.arange(DH)
    dd = d % 32
    inv = 1.0 / (10000.0 ** (2.0 * dd / DH))
    base = (2.0 * dd + 0.4 * DH) / (1.4 * DH)
    sgn = np.where(d < 32, -1.0, 1.0)

    def tab(tg, for_q):
        freqs = tg[:, None].astype(np.float64) * inv[None, :]
        power = (tg.astype(np.float64) - n // 2) / float(WIN // 2)
        scale = base[None, :] ** power[:, None]
        if not for_q:
            scale = 1.0 / scale
        return (np.cos(freqs) * scale, np.sin(freqs) * scale * sgn[None, :])

    def lay(a, nt):
        return np.ascontiguousarray(
            a.reshape(nt, 128, DH).transpose(1, 0, 2).reshape(128, nt * DH)
        ).astype(np.float32)

    cq, sq = tab(np.arange(T0, T0 + SPC), True)
    ck, sk = tab(np.clip(np.arange(T0 - WIN, T0 + SPC), 0, None), False)
    return (np.concatenate([lay(cq, TT), lay(sq, TT)], axis=1),
            np.concatenate([lay(ck, KTT), lay(sk, KTT)], axis=1))


def host_pack_dvec(vs):
    D = vs[0].shape[0]
    DCH = D // 128
    n = len(vs)
    out = np.zeros((128, DCH * n), np.float32)
    for i, v in enumerate(vs):
        for ch in range(DCH):
            out[:, ch * n + i] = v[128 * ch:128 * (ch + 1)]
    return out


def host_pack_taps(w):
    D, K = w.shape
    DCH = D // 128
    out = np.zeros((128, DCH * K), np.float32)
    for ch in range(DCH):
        out[:, ch * K:(ch + 1) * K] = w[128 * ch:128 * (ch + 1), :]
    return out


def host_tile_w(w, js):
    L, D, J = w.shape
    DCH = D // 128
    JP = js * 128
    wp = np.zeros((L, D, JP), np.float32)
    wp[:, :, :J] = w
    t = wp.reshape(L, DCH, 128, js, 128).transpose(0, 3, 2, 1, 4)
    return np.ascontiguousarray(t.reshape(L, js, 128, DCH * 128))


def host_tile_rows(w, fs):
    L, F, E2 = w.shape
    FP = fs * 128
    wp = np.zeros((L, FP, E2), np.float32)
    wp[:, :F, :] = w
    return np.ascontiguousarray(wp.reshape(L, fs, 128, E2))


class MK:
    def __init__(self, c):
        self.c = c
        for k, v in c.items():
            setattr(self, k, v)
        self.XO = 1
        self.nc = bass.Bass("TRN2", target_bir_lowering=False)
        self.fl = Flow()
        self.pgn = 0
        self.pgn2 = 0
        self.pgnF = 0
        self._nrm_ctr = 0

    def T_(self):
        return Tok()

    def npg(self):
        i = self.pgn % 3
        self.pgn += 1
        return self.PG[i], self.t_pg[i]

    def npgF(self):
        i = self.pgnF % 4
        self.pgnF += 1
        return self.PSC[i], self.t_psc[i]

    def npg2(self):
        i = self.pgn2 % 2
        self.pgn2 += 1
        return self.PG[i], self.t_pg[i]

    def sv(self, ch, i):
        return self.smallv[:, ch * self.VEC_N + i: ch * self.VEC_N + i + 1]

    def run_mm_plan(self, plan):
        pieces = []
        for (lo, hi, mk) in plan:
            while lo < hi:
                nb = min(hi, (lo // 512 + 1) * 512)
                pieces.append((lo, nb, mk))
                lo = nb
        first_bank = {}
        last_bank = {}
        for i, (lo, hi, mk) in enumerate(pieces):
            b = lo // 512
            if b not in first_bank:
                first_bank[b] = i
            last_bank[b] = i
        for i, (lo, hi, mk) in enumerate(pieces):
            b = lo // 512
            mk(lo, hi, first_bank[b] == i, last_bank[b] == i)

    def switch_region(self, rtok, tag):
        self.fl.op("V", lambda e: e.engine_nop(), writes=[rtok], tag="sw_" + tag)

    def build(self):
        self.declare_dram()
        from contextlib import ExitStack
        with ExitStack() as ctx:
            self.ctx = ctx
            self.declare_sbuf()
            self.phase_consts()
            for p in range(self.PASSES):
                self.PI = p
                self.switch_region(self.t_rB, "lnB")
                self.switch_region(self.t_rF, "lnF")
                self.switch_region(self.t_rA, "lnA")
                self.phase_module_ln()
                self.switch_region(self.t_rF, "convF")
                self.switch_region(self.t_rB, "convB")
                self.phase_recurrent()
                self.phase_layer(0)
                self.phase_ag1_and_s()
                self.phase_layer(1)
                self.phase_final()
            self.fl.check_deadlock()
            self.fl.emit(self.nc)
        return self.nc

    def declare_dram(self):
        nc = self.nc
        D, XE, INNER, ECH, JS, DCH = self.D, self.XE, self.INNER, self.ECH, self.JS, self.DCH
        TT, KTT, DH, WIN, NC_, SPC = self.TT, self.KTT, self.DH, self.WIN, self.NC, self.SPC
        NP, NSEG = self.PASSES, self.NSEG
        self.xT = nc.dram_tensor("xT", [NP, D, XE], F32, kind="ExternalInput")
        self.wqkv_d = nc.dram_tensor("wqkv", [2, D, 3 * INNER], F32, kind="ExternalInput")
        self.wo_d = nc.dram_tensor("wo_t", [2, ECH, 128, DCH * 128], F32, kind="ExternalInput")
        self.w1a_d = nc.dram_tensor("w1a_t", [2, JS, 128, DCH * 128], F32, kind="ExternalInput")
        self.w1g_d = nc.dram_tensor("w1g_t", [2, JS, 128, DCH * 128], F32, kind="ExternalInput")
        self.w2_d = nc.dram_tensor("w2_t", [2, JS, 128, DCH * 128], F32, kind="ExternalInput")
        self.outw_d = nc.dram_tensor("outw_t", [DCH, 128, DCH * 128], F32, kind="ExternalInput")
        self.gatew_d = nc.dram_tensor("gatew", [128, DCH * 7], F32, kind="ExternalInput")
        self.candw_d = nc.dram_tensor("candw", [128, DCH * 7], F32, kind="ExternalInput")
        self.dvec_d = nc.dram_tensor("dvecs", [128, DCH * self.VEC_N], F32, kind="ExternalInput")
        self.b1_d = nc.dram_tensor("b1pack", [128, 4 * JS], F32, kind="ExternalInput")
        self.qks_d = nc.dram_tensor("qkscale", [1, 4 * DH], F32, kind="ExternalInput")
        self.rowv_d = nc.dram_tensor("rowvec", [1, 4 * D], F32, kind="ExternalInput")
        self.tabq_d = nc.dram_tensor("tabq", [NP, 128, 2 * TT * DH], F32, kind="ExternalInput")
        self.tabk_d = nc.dram_tensor("tabk", [NP, 128, 2 * KTT * DH], F32, kind="ExternalInput")
        self.edge_d = nc.dram_tensor("edge_neg", [1, WIN], F32, kind="ExternalInput")
        self.carry_d = nc.dram_tensor("carry", [1, NP * 2 * NSEG], F32, kind="ExternalInput")
        self.onehot_d = nc.dram_tensor("onehot", [1, NP * NSEG], F32, kind="ExternalInput")
        self.prevrk_d = nc.dram_tensor("prevrk", [1, NP], I32, kind="ExternalInput")
        self.bcol_d = nc.dram_tensor("bcol", [128, NP], F32, kind="ExternalInput")
        self.oT = nc.dram_tensor("oT", [D, NP * SPC], F32, kind="ExternalOutput")

        self.ag0_in = [nc.dram_tensor("ag0i%d" % l, [128, 2 * DCH], F32)
                       for l in range(2)]
        self.ag0_out = [nc.dram_tensor("ag0o%d" % l, [NSEG, 128, 2 * DCH], F32)
                        for l in range(2)]
        self.AG1W = DCH * (WIN + 1)
        self.ag1_in = nc.dram_tensor("ag1i", [128, self.AG1W], BF16)
        self.ag1_out = nc.dram_tensor("ag1o", [NSEG, 128, self.AG1W], BF16)
        self.hg_dram = nc.dram_tensor("hgd", [JS, 128, SPC], BF16)
        self.h_dram = nc.dram_tensor("hd", [DCH, 128, SPC], BF16)
        self.xn_dram = nc.dram_tensor("xnd", [DCH, 128, XE], BF16)
        self.dbg_dram = [nc.dram_tensor("dbg%d" % i, [NP, DCH, 128, SPC + 1], F32)
                         for i in range(4)]
        self.y_dram = nc.dram_tensor("yd", [DCH, 128, SPC], F32)

        i = np.arange(128)[:, None]
        j = np.arange(128)[None, :]
        tri_lo = np.where(j < i, NEG, 0.0).astype(np.float32)
        tri_hi = np.where(j > i, NEG, 0.0).astype(np.float32)
        bf = ml_dtypes.bfloat16
        self.i_idn = nc.inline_tensor(np.eye(128, dtype=bf), "c_idn")
        self.i_idn32 = nc.inline_tensor(np.eye(128, dtype=np.float32), "c_idn32")
        self.i_onescb = nc.inline_tensor(np.ones((128, 1), dtype=bf), "c_o1c")
        self.i_onesrb = nc.inline_tensor(np.ones((1, 128), dtype=bf), "c_o1r")
        self.i_onesw32 = nc.inline_tensor(np.ones((33, 128), dtype=np.float32), "c_ow32")
        self.i_ones512 = nc.inline_tensor(np.ones((1, 512), dtype=np.float32),
                                          "c_o512")
        self.i_negw32 = nc.inline_tensor(np.full((1, 512), -1.0, dtype=np.float32),
                                         "c_nw32")
        self.i_oneswb = nc.inline_tensor(np.ones((1, max(WIN, 512)), dtype=bf), "c_owb")
        self.i_tlo = nc.inline_tensor(tri_lo.astype(bf), "c_tlo")
        self.i_thi = nc.inline_tensor(tri_hi.astype(bf), "c_thi")
        self.i_tloT = nc.inline_tensor(np.ascontiguousarray(tri_lo.T).astype(bf), "c_tloT")
        self.i_thiT = nc.inline_tensor(np.ascontiguousarray(tri_hi.T).astype(bf), "c_thiT")
        self.i_iota = nc.inline_tensor(
            np.arange(1, SPC + 1, dtype=np.float32).reshape(1, SPC), "c_iota")
        cv = np.zeros((128, 7), np.float32)
        cv[:, 0] = self.LN_EPS
        cv[:, 1] = self.RMS_EPS
        cv[:, 2] = -0.5
        cv[:, 3] = 1.0 / self.D
        cv[:, 4] = 1e-24
        cv[:, 5] = -20.0
        cv[:, 6] = -1.0
        self.i_cval = nc.inline_tensor(cv, "c_cval")

    def declare_sbuf(self):
        nc, E = self.nc, self.ctx.enter_context
        DCH, SPC, WIN, INNER, ECH, JS = (self.DCH, self.SPC, self.WIN, self.INNER,
                                         self.ECH, self.JS)
        XE, KTT, TT, DH, NC_, BW, TCW = (self.XE, self.KTT, self.TT, self.DH,
                                         self.NC, self.BW, self.TCW)

        def sb(name, shape, dt):
            return E(nc.sbuf_tensor(name, shape, dt))

        self.x1 = [sb("x1_%d" % ch, [128, SPC + 1], F32) for ch in range(DCH)]
        self.ofm = [[sb("ofm%d_%d" % (s, e2), [128, WIN], BF16) for e2 in range(ECH)]
                    for s in range(1)]
        self.smallv = sb("smallv", [128, DCH * self.VEC_N], F32)
        self.gwsb = sb("gwsb", [128, DCH * 7], F32)
        self.cwsb = sb("cwsb", [128, DCH * 7], F32)
        self.b1sb = sb("b1sb", [128, 4 * JS], F32)
        self.qkssb = sb("qkssb", [1, 4 * DH], F32)
        self.qksrot = sb("qksrot", [1, 4 * DH], F32)
        self.idsb = sb("idsb", [128, 128], BF16)
        self.id32sb = sb("id32sb", [128, 128], F32)
        self.onescb = sb("onescb", [128, 1], BF16)
        self.onesrb = sb("onesrb", [1, 128], BF16)
        self.onesw32 = sb("onesw32", [33, 128], F32)
        self.ones512 = sb("ones512", [1, 512], F32)
        self.oneswb = sb("oneswb", [1, max(WIN, 512)], BF16)
        self.tloTsb = sb("tloTsb", [128, 128], BF16)
        self.thiTsb = sb("thiTsb", [128, 128], BF16)
        self.cvalsb = sb("cvalsb", [128, 7], F32)
        self.edgebf = sb("edgebf", [1, WIN], BF16)
        self.carrysb = sb("carrysb", [1, self.PASSES * 2 * self.NSEG], F32)
        self.onehotsb = sb("onehotsb", [1, self.PASSES * self.NSEG], F32)
        self.prevsb = sb("prevsb", [1, self.PASSES], I32)
        self.bcolb = sb("bcolb", [128, self.PASSES], BF16)
        self.wq_sb = [sb("wq_%d" % ch, [128, 3 * INNER], BF16) for ch in range(DCH)]
        self.wo_sb = [sb("wo_%d" % o, [128, DCH * 128], BF16) for o in range(ECH)]
        self.w1_sb = [sb("w1s_%d" % i, [128, DCH * 128], BF16) for i in range(2)]
        self.rstd = sb("rstd", [128, 4 * self.H], F32)
        self.bstage = sb("bstage", [128, DCH], F32)
        self.fstage = sb("fstage", [128, DCH], F32)
        self.bsel = sb("bsel", [128, DCH * self.NSEG], F32)
        self.fincol = sb("fincol", [128, self.NSEG * DCH], F32)
        self.cwt = sb("cwt", [128, self.NSEG], F32)
        self.hincol = sb("hincol", [128, DCH], F32)
        self.stg1 = sb("stg1", [128, TCW], F32)
        self.stg2 = sb("stg2", [128, TCW], F32)
        self.dgw = [sb("dgw%d" % t, [128, 128], BF16) for t in range(7)]
        self.dcw = [sb("dcw%d" % t, [128, 128], BF16) for t in range(7)]
        self.decayc = sb("decayc", [128, DCH], F32)
        rows1 = sb("rows1", [33, XE], F32)
        rows2 = sb("rows2", [1, XE], F32)
        rows3 = sb("rows3", [1, XE], F32)
        self.statr_sum = rows1[0:1, :]
        self.murow = rows1[0:1, :]
        self.rmsrow = rows1[32:33, 0:SPC]
        self.statr_sq = rows2[0:1, :]
        self.rsrow = rows2[0:1, :]
        self.mrrow = rows3[0:1, :]
        self.lmu = self.murow[:, 0:SPC]
        self.lrs = self.rsrow[:, 0:SPC]
        self.lmr = self.mrrow[:, 0:SPC]


        def tenant(sizes):
            cur = 0
            for s in sizes:
                cur = (cur + 255) // 256 * 256 + s
            return cur

        szA = max(tenant([XE] * DCH), tenant([SPC] * (DCH + 1)))
        regA = Region(sb("regA", [128, szA], BF16), szA, BF16)
        self.xn = [regA.alloc(128, XE, BF16) for _ in range(DCH)]
        regA.reset()
        self.xln = [regA.alloc(128, SPC, BF16) for _ in range(DCH)]
        self.glv = regA.alloc(128, SPC, BF16)
        regA.reset()
        self.x1b = [regA.alloc(128, XE, BF16) for _ in range(DCH)]

        self.JSH = (JS + 1) // 2
        nwt = WIN // 128
        szB = max(
            tenant([WIN] * (2 * ECH) + [INNER] * (2 * nwt) + [WIN] * ECH
                   + [INNER] * (2 * nwt) + [WIN] * 10
                   + [INNER] * 5 + [2 * TT * DH, 2 * KTT * DH]),
            tenant([XE]),
            tenant([SPC]),
            tenant([SPC] * 4 + [DCH * 128] * self.JSH + [TCW] * self.JSH
                   + [TCW] * 4 + [DCH * 128] * 4),
            tenant([SPC] * DCH + [DCH * 128] * DCH + [TCW] * 3))
        regB = Region(sb("regB", [128, szB], BF16), szB, BF16)
        self.kfm = [[regB.alloc(128, WIN, BF16) for _ in range(ECH)] for _ in range(2)]
        self.vtm = [[regB.alloc(128, INNER, BF16) for _ in range(WIN // 128)]
                    for _ in range(2)]
        self.qfm = [[regB.alloc(128, WIN, BF16) for _ in range(ECH)] for _ in range(1)]
        self.qtm = [regB.alloc(128, INNER, BF16) for _ in range(WIN // 128)]
        self.ktm = [regB.alloc(128, INNER, BF16) for _ in range(WIN // 128)]
        self.wt = [[regB.alloc(128, WIN, BF16) for _ in range(4)]
                   for _ in range(2)]
        self.rt1 = [regB.alloc(128, INNER, BF16) for _ in range(2)]
        self.zbb = [regB.alloc(128, WIN, BF16) for _ in range(2)]
        self.qn = []
        self.qn_off = []
        for _ in range(2):
            self.qn.append(regB.alloc(128, INNER, BF16))
            self.qn_off.append(regB.last_off)
        self.qn_h, self.szB = regB.h, szB
        self.effq = regB.alloc(128, 2 * TT * DH, BF16)
        self.effq_off = regB.last_off
        self.effk = regB.alloc(128, 2 * KTT * DH, BF16)
        self.effk_off = regB.last_off
        usedB = regB.cur
        regB.reset()
        self.sqtmp = regB.alloc(128, XE, BF16)
        usedB = max(usedB, regB.cur)
        regB.reset()
        self.hbf = regB.alloc(128, SPC, BF16)
        usedB = max(usedB, regB.cur)
        regB.reset()
        self.hgst = [[regB.alloc(128, SPC, BF16) for _ in range(2)]
                     for _ in range(2)]
        self.w2t = [regB.alloc(128, DCH * 128, BF16) for _ in range(self.JSH)]
        self.hgw = [regB.alloc(128, TCW, BF16) for _ in range(self.JSH)]
        self.ysq_f = [regB.alloc(128, TCW, BF16) for _ in range(2)]
        self.sqb2_f = [regB.alloc(128, TCW, BF16) for _ in range(2)]
        for _ in range(4):
            self.w1_sb.append(regB.alloc(128, DCH * 128, BF16))
        usedB = max(usedB, regB.cur)
        regB.reset()
        self.hres = [regB.alloc(128, SPC, BF16) for _ in range(DCH)]
        self.owf = [regB.alloc(128, DCH * 128, BF16) for _ in range(DCH)]
        self.xnre = [regB.alloc(128, TCW, BF16) for _ in range(2)]
        self.ysq_s = regB.alloc(128, TCW, BF16)
        usedB = max(usedB, regB.cur)
        assert usedB <= szB, (usedB, szB)

        CW = min(512, XE)
        szF = max(tenant([XE]), tenant([CW] * 2), tenant([SPC] * 3),
                  tenant([2 * WIN, INNER, INNER]),
                  tenant([TCW, SPC]))
        regF = Region(sb("regF", [128, szF], F32), szF, F32)
        self.xfp0 = regF.alloc(128, XE, F32)
        regF.reset()
        self.xfp = [regF.alloc(128, CW, F32), regF.alloc(128, CW, F32)]
        usedF = max(regF.cur, XE)
        regF.reset()
        self.gatef = regF.alloc(128, SPC, F32)
        self.candf = regF.alloc(128, SPC, F32)
        self.hstr = regF.alloc(128, SPC, F32)
        usedF = max(usedF, regF.cur)
        regF.reset()
        self.rowfl = regF.alloc(1, 2 * WIN, F32)
        self.qsq = [regF.alloc(128, INNER, F32) for _ in range(2)]
        usedF = max(usedF, regF.cur)
        regF.reset()
        self.powt = regF.alloc(128, TCW, F32)
        self.iotasb = regF.alloc(1, SPC, F32)
        usedF = max(usedF, regF.cur)
        regF.reset()
        self.yre = regF.alloc(128, SPC, F32)
        self.xre = regF.alloc(128, SPC, F32)
        self.obuf = regF.alloc(128, SPC, F32)
        usedF = max(usedF, regF.cur)
        assert usedF <= szF, usedF

        self.PG = [E(nc.psum_tensor("pg%d" % i, [128, 512], F32)) for i in range(3)]
        self.PSC = [E(nc.psum_tensor("psc%d" % i, [128, 512], F32))
                    for i in range(4)]
        self.PTB = E(nc.psum_tensor("ptb", [128, 1024], BF16))

        T = self.T_
        self.t_rA, self.t_rB = T(), T()
        self.t_x1 = [T() for _ in range(DCH)]
        self.t_ofm = [T() for _ in range(1)]
        self.t_const = T()
        self.t_small, self.t_gw, self.t_cw, self.t_b1 = T(), T(), T(), T()
        self.t_qks, self.t_qksr, self.t_rowv = T(), T(), T()
        self.t_wq = [T() for _ in range(DCH)]
        self.t_wo = [T() for _ in range(ECH)]
        self.t_w1 = [T() for _ in range(6)]
        self.t_eff = T()
        self.t_hbf = T()
        self.t_xnre = [T(), T()]
        self.t_rowfl = T()
        self.t_bstage, self.t_fstage, self.t_bsel = T(), T(), T()
        self.t_fincol, self.t_cwt, self.t_hincol = T(), T(), T()
        self.t_powt, self.t_stg1, self.t_stg2, self.t_ysq, self.t_sqb2 = \
            T(), T(), T(), T(), T()
        self.t_dgw = [T() for _ in range(7)]
        self.t_dcw = [T() for _ in range(7)]
        self.t_decay, self.t_rmsrow, self.t_statr = T(), T(), T()
        self.t_statq = T()
        
        self.t_xn = [T() for _ in range(DCH)]
        self.t_xln = [T() for _ in range(DCH)]
        self.t_gl = T()
        self.t_x1b = [T() for _ in range(DCH)]
        self.t_kfm = [T() for _ in range(3)]
        self.t_vtm = [T() for _ in range(3)]
        self.t_qfm = [T() for _ in range(1)]
        self.t_qtm, self.t_ktm = T(), T()
        self.t_wt = [[T(), T(), T(), T()] for _ in range(2)]
        self.t_qsq = [T(), T()]
        self.t_qn = [T(), T()]
        self.t_rt1 = [T(), T()]
        self.t_rstd = [T(), T()]
        self.t_zbb = [T(), T()]
        self.t_ysqf = [T(), T()]
        self.t_sqb2f = [T(), T()]
        self.t_xfp = [T()]
        self.t_xfp2 = [T(), T()]
        self.t_rF = T()
        self.t_rsrow, self.t_mrrow, self.t_sqtmp = T(), T(), T()
        self.t_gate, self.t_cand, self.t_hstr = T(), T(), T()
        self.t_hgst = [[T(), T()] for _ in range(2)]
        self.t_w2t = [T() for _ in range(JS)]
        self.t_hgw = [T() for _ in range(self.JSH)]
        self.t_hres = [T() for _ in range(DCH)]
        self.t_owf = [T() for _ in range(DCH)]
        self.t_yre, self.t_xre, self.t_obuf = T(), T(), T()
        self.t_pg = [T() for _ in range(3)]
        self.t_ptb = T()
        self.t_psc = [T() for _ in range(4)]
        self.t_ag0i = [T() for _ in range(2)]
        self.t_ag0o = [T() for _ in range(2)]
        self.t_ag1i, self.t_ag1o = T(), T()
        self.t_murow = self.t_statr
        self.t_hgd = [T() for _ in range(JS)]
        self.t_hd = [T() for _ in range(DCH)]
        self.t_xnd = [T() for _ in range(DCH)]
        self.t_yd = [T() for _ in range(DCH)]
        self.t_fin = T()
        self.t_pass, self.t_iota = T(), T()

    def phase_consts(self):
        fl = self.fl
        tr = []
        for dst, src in [
                (self.idsb, self.i_idn), (self.id32sb, self.i_idn32),
                (self.onescb, self.i_onescb),
                (self.onesrb, self.i_onesrb), (self.onesw32, self.i_onesw32),
                (self.ones512, self.i_ones512),
                (self.oneswb, self.i_oneswb), (self.tloTsb, self.i_tloT),
                (self.thiTsb, self.i_thiT),
                (self.cvalsb, self.i_cval),
                (self.prevsb, self.prevrk_d),
                (self.carrysb, self.carry_d), (self.onehotsb, self.onehot_d),
                (self.smallv, self.dvec_d), (self.gwsb, self.gatew_d),
                (self.cwsb, self.candw_d), (self.b1sb, self.b1_d),
                (self.qkssb, self.qks_d)]:
            tr.append(lambda e, dst=dst, src=src:
                      e.dma_start(out=dst[:, :], in_=src[:, :]))
        fl.dma("Q", tr, writes=[self.t_const, self.t_small, self.t_gw, self.t_cw,
                                self.t_b1, self.t_qks], tag="const")
        fl.dma("G", [
            lambda e: e.dma_start(out=self.edgebf[:, :], in_=self.edge_d[:, :]),
            lambda e: e.dma_start(out=self.bcolb[:, :], in_=self.bcol_d[:, :]),
        ], writes=[self.t_const], tag="constg")
        fl.op("S", lambda e: e.activation(
            self.decayc[:, :],
            self.smallv[:, :].rearrange("p (c n) -> p c n", n=self.VEC_N)[:, :, 2],
            AF.Exp), reads=[self.t_small], writes=[self.t_decay], tag="decay")

        def qksrot_fn(e):
            DH = self.DH
            src = bass.AP(self.qkssb, 32, [[4 * DH, 1], [DH, 4], [-32, 2], [1, 32]])
            return e.tensor_copy(
                self.qksrot[:, :].rearrange("p (r x i) -> p r x i", r=4, x=2), src)
        fl.op("V", qksrot_fn, reads=[self.t_qks], writes=[self.t_qksr], tag="qksrot")

    def phase_module_ln(self):
        fl, D, DCH, XE = self.fl, self.D, self.DCH, self.XE
        rB, rA = self.t_rB, self.t_rA
        nst = (XE + 511) // 512
        PI = self.PI
        for ch in range(DCH):
            fl.dma("Q", [lambda e, ch=ch, PI=PI: e.dma_start(
                out=self.xfp0, in_=self.xT[PI, 128 * ch:128 * (ch + 1), :])],
                reads=[self.t_rF, self.t_xfp[0]], writes=[self.t_xfp[0]], tag="ldx")
            fl.op("V", lambda e, ch=ch: e.tensor_copy(self.xn[ch], self.xfp0),
                  reads=[self.t_xfp[0], rA, self.t_rF], writes=[self.t_xn[ch]],
                  tag="castx")
        for t in range(nst):
            lo = t * 512
            w = min(XE, lo + 512) - lo
            p0, pt0 = self.PG[0], self.t_pg[0]
            p1, pt1 = self.PG[1], self.t_pg[1]
            for ch in range(DCH):
                fl.op("S", lambda e, ch=ch, lo=lo, w=w: e.square(
                    self.sqtmp[:, lo:lo + w], self.xn[ch][:, lo:lo + w]),
                    reads=[self.t_xn[ch], rB], writes=[self.t_sqtmp], tag="sq")
                fl.op("T", lambda e, ch=ch, lo=lo, w=w, p0=p0: e.matmul(
                    p0[0:1, 0:w], self.onescb[:, :], self.xn[ch][:, lo:lo + w],
                    start=(ch == 0), stop=(ch == DCH - 1)),
                    reads=[self.t_xn[ch], self.t_const], writes=[pt0], tag="stsum")
                fl.op("T", lambda e, ch=ch, lo=lo, w=w, p1=p1: e.matmul(
                    p1[0:1, 0:w], self.onescb[:, :], self.sqtmp[:, lo:lo + w],
                    start=(ch == 0), stop=(ch == DCH - 1)),
                    reads=[self.t_sqtmp, self.t_const], writes=[pt1], tag="stsq")
            fl.op("S", lambda e, lo=lo, w=w, p0=p0: e.copy(
                self.statr_sum[:, lo:lo + w], p0[0:1, 0:w]),
                reads=[pt0], writes=[self.t_statr], tag="strow0")
            fl.op("S", lambda e, lo=lo, w=w, p1=p1: e.copy(
                self.statr_sq[:, lo:lo + w], p1[0:1, 0:w]),
                reads=[pt1], writes=[self.t_rsrow], tag="strow1")
        fl.op("V", lambda e: e.tensor_scalar_mul(self.murow, self.statr_sum,
                                                 1.0 / D),
              reads=[rB], writes=[self.t_murow], tag="mu")
        fl.op("V", lambda e: e.tensor_mul(self.mrrow, self.murow, self.murow),
              reads=[self.t_murow, rB], writes=[self.t_mrrow], tag="mu2")
        fl.op("V", lambda e: e.scalar_tensor_tensor(
            self.rsrow, self.statr_sq, 1.0 / D, self.mrrow,
            ALU.mult, ALU.subtract),
            reads=[self.t_mrrow, rB], writes=[self.t_rsrow], tag="var")
        fl.op("S", lambda e: e.activation(self.rsrow, self.rsrow, AF.Ln,
                                          bias=self.cvalsb[0:1, 0:1]),
              reads=[rB], writes=[self.t_rsrow], tag="lnv")
        fl.op("S", lambda e: e.activation(self.rsrow, self.rsrow, AF.Exp, scale=self.cvalsb[0:1, 2:3]),
              reads=[rB], writes=[self.t_rsrow], tag="rstd")
        fl.op("V", lambda e: e.tensor_mul(self.mrrow, self.murow, self.rsrow),
              reads=[self.t_murow, self.t_rsrow, rB], writes=[self.t_mrrow], tag="murs")
        for t in range(nst):
            lo = t * 512
            w = min(XE, lo + 512) - lo
            pa, ta = self.npg()
            pb, tb = self.npg()
            fl.op("T", lambda e, lo=lo, w=w, pa=pa: e.matmul(
                pa[:, 0:w], self.onesw32[0:1, 0:128], self.rsrow[:, lo:lo + w],
                start=True, stop=True),
                reads=[self.t_rsrow, self.t_const, rB], writes=[ta], tag="rsB")
            fl.op("T", lambda e, lo=lo, w=w, pb=pb: e.matmul(
                pb[:, 0:w], self.onesw32[0:1, 0:128], self.mrrow[:, lo:lo + w],
                start=True, stop=True),
                reads=[self.t_mrrow, self.t_const, rB], writes=[tb], tag="mrB")
            for ch in range(DCH):
                sl = ch % 2
                fl.dma("Q", [lambda e, ch=ch, sl=sl, lo=lo, w=w, PI=PI:
                             e.dma_start(
                                 out=self.xfp[sl][:, 0:w],
                                 in_=self.xT[PI, 128 * ch:128 * (ch + 1),
                                             lo:lo + w])],
                    reads=[self.t_rF],
                    writes=[self.t_xfp2[sl], self.t_xfp[0]], tag="ldx2")
                fl.op("V", lambda e, sl=sl, w=w, pa=pa: e.tensor_mul(
                    self.xfp[sl][:, 0:w], self.xfp[sl][:, 0:w], pa[:, 0:w]),
                    reads=[ta, self.t_rF], writes=[self.t_xfp2[sl]], tag="xr")
                fl.op("V", lambda e, ch=ch, sl=sl, lo=lo, w=w, pb=pb: e.tensor_sub(
                    self.xn[ch][:, lo:lo + w], self.xfp[sl][:, 0:w], pb[:, 0:w]),
                    reads=[self.t_xfp2[sl], tb, rA, self.t_rF],
                    writes=[self.t_xn[ch]], tag="xnw")
        for ch in range(DCH):
            fl.dma("Q", [lambda e, ch=ch: e.dma_start(
                out=self.xn_dram[ch, :, :], in_=self.xn[ch])],
                reads=[self.t_xn[ch], rA], writes=[self.t_xnd[ch]], tag="xnsp")

    def phase_recurrent(self):
        fl, DCH, TC, TCW, LEXT, SPC = (self.fl, self.DCH, self.TC, self.TCW,
                                       self.LEXT, self.SPC)
        rB, rA = self.t_rF, self.t_rA
        for ch in range(DCH):
            for tp in range(7):
                fl.op("V", lambda e, ch=ch, tp=tp: e.tensor_scalar_mul(
                    self.dgw[tp][:, :], self.idsb[:, :],
                    self.gwsb[:, ch * 7 + tp:ch * 7 + tp + 1]),
                    reads=[self.t_const, self.t_gw], writes=[self.t_dgw[tp]], tag="dgw")
                fl.op("V", lambda e, ch=ch, tp=tp: e.tensor_scalar_mul(
                    self.dcw[tp][:, :], self.idsb[:, :],
                    self.cwsb[:, ch * 7 + tp:ch * 7 + tp + 1]),
                    reads=[self.t_const, self.t_cw], writes=[self.t_dcw[tp]], tag="dcw")
            for t in range(TC):
                base = LEXT + t * TCW
                pg_, tg_ = self.npg()
                pc_, tc_ = self.npg()
                for tp in range(7):
                    b2 = base + tp - 3
                    fl.op("T", lambda e, ch=ch, tp=tp, b2=b2, pg_=pg_: e.matmul(
                        pg_[:, 0:TCW], self.dgw[tp][:, :], self.xn[ch][:, b2:b2 + TCW],
                        start=(tp == 0), stop=(tp == 6)),
                        reads=[self.t_dgw[tp], self.t_xn[ch], rA],
                        writes=[tg_], tag="cvg")
                fl.op("S", lambda e, ch=ch, t=t, pg_=pg_: e.activation(
                    self.gatef[:, t * TCW:(t + 1) * TCW], pg_[:, 0:TCW], AF.Sigmoid,
                    bias=self.sv(ch, 0)),
                    reads=[tg_, self.t_small, rB], writes=[self.t_gate], tag="sig")
                for tp in range(7):
                    b2 = base + tp - 3
                    fl.op("T", lambda e, ch=ch, tp=tp, b2=b2, pc_=pc_: e.matmul(
                        pc_[:, 0:TCW], self.dcw[tp][:, :], self.xn[ch][:, b2:b2 + TCW],
                        start=(tp == 0), stop=(tp == 6)),
                        reads=[self.t_dcw[tp], self.t_xn[ch], rA],
                        writes=[tc_], tag="cvc")
                fl.op("S", lambda e, ch=ch, t=t, pc_=pc_: e.activation(
                    self.candf[:, t * TCW:(t + 1) * TCW], pc_[:, 0:TCW], AF.Tanh,
                    bias=self.sv(ch, 1)),
                    reads=[tc_, self.t_small, rB], writes=[self.t_cand], tag="tanh")
            fl.op("V", lambda e: e.tensor_mul(self.gatef, self.gatef, self.candf),
                  reads=[self.t_cand, rB], writes=[self.t_gate], tag="gated")
            fl.op("V", lambda e, ch=ch: e.tensor_tensor_scan(
                self.hstr[:, :], self.decayc[:, ch:ch + 1].broadcast_to([128, SPC]),
                self.gatef, 0.0, ALU.mult, ALU.add),
                reads=[self.t_gate, self.t_decay, rB], writes=[self.t_hstr], tag="scan")
            fl.op("V", lambda e, ch=ch: e.tensor_copy(
                self.fstage[:, ch:ch + 1], self.hstr[:, SPC - 1:SPC]),
                reads=[self.t_hstr, rB], writes=[self.t_fstage], tag="Fcol")
            fl.op("V", lambda e: e.tensor_copy(self.hbf[:, :], self.hstr[:, :]),
                  reads=[self.t_hstr, rB, self.t_rB], writes=[self.t_hbf], tag="hcast")
            fl.dma("Q", [lambda e, ch=ch: e.dma_start(
                out=self.h_dram[ch, :, :], in_=self.hbf[:, :])],
                reads=[self.t_hbf, self.t_rB], writes=[self.t_hd[ch]], tag="hsp")

    def x_src(self, l):
        if l == 0:
            return self.xn, self.t_xn
        return self.x1b, self.t_x1b

    def load_layer_weights(self, l):
        fl, DCH, ECH, DH, TT, KTT = self.fl, self.DCH, self.ECH, self.DH, self.TT, self.KTT
        fl.dma("G", [lambda e, ch=ch, l=l: e.dma_start(
            out=self.wq_sb[ch][:, :], in_=self.wqkv_d[l, 128 * ch:128 * (ch + 1), :])
            for ch in range(DCH)],
            writes=[self.t_wq[i] for i in range(DCH)], tag="wqL%d" % l)
        fl.dma("G", [lambda e, o=o, l=l: e.dma_start(
            out=self.wo_sb[o][:, :], in_=self.wo_d[l, o, :, :])
            for o in range(ECH)],
            writes=[self.t_wo[o] for o in range(ECH)], tag="woL%d" % l)
        PI, D = self.PI, self.D

    def emit_eff(self, l):
        fl, DH, TT, KTT, PI = self.fl, self.DH, self.TT, self.KTT, self.PI
        fl.dma("G", [
            lambda e, PI=PI: e.dma_start(out=self.effq[:, :], in_=self.tabq_d[PI, :, :]),
            lambda e, PI=PI: e.dma_start(out=self.effk[:, :], in_=self.tabk_d[PI, :, :]),
        ], reads=[self.t_rB], writes=[self.t_eff], tag="tabL%d" % l)
        for (dst, half, ntt, srow, rot, extra) in [
                (self.effq, 0, TT, 2 * l + 0, False, self.QK_SCALE),
                (self.effq, 1, TT, 2 * l + 0, True, self.QK_SCALE),
                (self.effk, 0, KTT, 2 * l + 1, False, 1.0),
                (self.effk, 1, KTT, 2 * l + 1, True, 1.0)]:
            ssb = self.qksrot if rot else self.qkssb
            stok = self.t_qksr if rot else self.t_qks
            col = half * ntt * DH
            pq, pqt = self.npg()
            fl.op("T", lambda e, pq=pq, ssb=ssb, srow=srow: e.matmul(
                pq[:, 0:DH], self.onesw32[0:1, 0:128],
                ssb[0:1, srow * DH:(srow + 1) * DH],
                start=True, stop=True),
                reads=[stok, self.t_const], writes=[pqt], tag="qsB")

            def f(e, dst=dst, col=col, ntt=ntt, pq=pq, extra=extra):
                b = bass.AP(pq, 0, [[512, 128], [0, ntt], [1, DH]])
                return e.scalar_tensor_tensor(
                    dst[:, col:col + ntt * DH].rearrange("p (t d) -> p t d", d=DH),
                    dst[:, col:col + ntt * DH].rearrange("p (t d) -> p t d", d=DH),
                    float(extra), b, ALU.mult, ALU.mult)
            fl.op("V", f, reads=[pqt, self.t_rB], writes=[self.t_eff], tag="eff")

    def emit_qkv_tile(self, l, kv_i, own_t):
        fl = self.fl
        DCH, INNER, WIN = self.DCH, self.INNER, self.WIN
        X, tX = self.x_src(l)
        ext = self.LEXT - WIN + 128 * kv_i
        wslot = (kv_i // (WIN // 128)) % 2
        tloc = kv_i % (WIN // 128)
        plan = ([("q", 0)] if own_t is not None else []) + \
            [("k", INNER), ("v", 2 * INNER)]
        for name, eoff in plan:
            p, pt = self.npg()
            for chd in range(DCH):
                sh = 1 if chd >= DCH // 2 else 0
                fl.op("T", lambda e, chd=chd, eoff=eoff, ext=ext, sh=sh, p=p:
                      e.matmul(p[:, 0:INNER],
                               X[chd][:, ext - sh:ext - sh + 128],
                               self.wq_sb[chd][:, eoff:eoff + INNER],
                               start=(chd == 0), stop=(chd == DCH - 1)),
                      reads=[tX[chd], self.t_wq[chd], self.t_rA], writes=[pt],
                      tag="qkv")
            if name == "v":
                fl.op("V", lambda e, p=p, wslot=wslot, tloc=tloc: e.tensor_copy(
                    self.vtm[wslot][tloc][:, :], p[:, 0:INNER]),
                    reads=[pt, self.t_rB], writes=[self.t_vtm[wslot]], tag="vev")
            else:
                self.emit_qk_norm_rot(name, p, pt, kv_i, own_t, tloc)

    def emit_qk_norm_rot(self, name, p, pt, kv_i, own_t, tloc):
        fl = self.fl
        H, DH, INNER, WIN = self.H, self.DH, self.INNER, self.WIN
        rB = self.t_rB
        is_q = name == "q"
        pr = self._nrm_ctr % 2
        self._nrm_ctr += 1
        ro = (0 if is_q else H) + 2 * H * pr
        qsq, tqsq = self.qsq[pr], self.t_qsq[pr]
        qn, tqn, qn_off = self.qn[pr], self.t_qn[pr], self.qn_off[pr]
        rt1, trt1 = self.rt1[pr], self.t_rt1[pr]
        trstd = self.t_rstd[pr]
        fl.op("S", lambda e, p=p, qsq=qsq: e.square(qsq[:, :], p[:, 0:INNER]),
              reads=[pt, rB, self.t_rF], writes=[tqsq], tag="l2sq")
        fl.op("V", lambda e, ro=ro, qsq=qsq: e.reduce_sum(
            self.rstd[:, ro:ro + H],
            qsq[:, :].rearrange("p (h d) -> p h d", d=DH), axis=AX.X),
            reads=[tqsq, rB, self.t_rF], writes=[trstd], tag="l2red")
        fl.op("S", lambda e, ro=ro: e.activation(
            self.rstd[:, ro:ro + H], self.rstd[:, ro:ro + H], AF.Ln,
            bias=self.cvalsb[:, 4:5]),
            reads=[self.t_const], writes=[trstd], tag="l2ln")
        fl.op("S", lambda e, ro=ro: e.activation(
            self.rstd[:, ro:ro + H], self.rstd[:, ro:ro + H], AF.Exp,
            scale=self.cvalsb[:, 2:3]),
            writes=[trstd], tag="l2ex")

        def qn_fn(e, p=p, ro=ro, qn=qn):
            b = bass.AP(self.rstd, ro, [[4 * H, 128], [1, H], [0, DH]])
            return e.tensor_tensor(
                qn[:, :].rearrange("p (h d) -> p h d", d=DH),
                p[:, 0:INNER].rearrange("p (h d) -> p h d", d=DH), b, ALU.mult)
        fl.op("V", qn_fn, reads=[pt, trstd, rB], writes=[tqn], tag="qn")

        eff_off = self.effq_off if is_q else self.effk_off
        ntt = self.TT if is_q else self.KTT
        ti = own_t if is_q else kv_i
        cs_off = ti * DH
        ss_off = ntt * DH + ti * DH

        def rot_fn(e, eff_off=eff_off, ss_off=ss_off, qn_off=qn_off, rt1=rt1):
            rotsrc = bass.AP(self.qn_h, qn_off + 32,
                             [[self.szB, 128], [DH, H], [-32, 2], [1, 32]])
            b = bass.AP(self.qn_h, eff_off + ss_off,
                        [[self.szB, 128], [0, H], [32, 2], [1, 32]])
            return e.tensor_tensor(
                rt1[:, :].rearrange("p (h x i) -> p h x i", x=2, i=32),
                rotsrc, b, ALU.mult)
        fl.op("V", rot_fn, reads=[tqn, self.t_eff, rB], writes=[trt1],
              tag="rot")

        def cs_fn(e, eff_off=eff_off, cs_off=cs_off, qn=qn):
            b = bass.AP(self.qn_h, eff_off + cs_off,
                        [[self.szB, 128], [0, H], [1, DH]])
            return e.tensor_tensor(
                qn[:, :].rearrange("p (h d) -> p h d", d=DH),
                qn[:, :].rearrange("p (h d) -> p h d", d=DH), b, ALU.mult)
        fl.op("V", cs_fn, reads=[self.t_eff, rB], writes=[tqn], tag="rcs")
        if is_q:
            dst, dtok = self.qtm[own_t % (WIN // 128)], self.t_qtm
        else:
            dst, dtok = self.ktm[tloc], self.t_ktm
        fl.op("V", lambda e, dst=dst, qn=qn, rt1=rt1: e.tensor_add(
            dst[:, :], qn[:, :], rt1[:, :]),
            reads=[tqn, trt1, rB], writes=[dtok], tag="radd")

    def emit_fm_transpose(self, which, wslot, qslot):
        fl, ECH, WIN = self.fl, self.ECH, self.WIN
        ntile = WIN // 128
        if which == "q":
            src, stok = self.qtm, self.t_qtm
            dsts, dtok = self.qfm[qslot], self.t_qfm[qslot]
        else:
            src, stok = self.ktm, self.t_ktm
            dsts, dtok = self.kfm[wslot], self.t_kfm[wslot]
        for eb in range(ECH):
            p, pt = self.PTB, self.t_ptb
            for tt in range(ntile):
                fl.op("T", lambda e, eb=eb, tt=tt, p=p: e.transpose(
                    p[:, tt * 128:(tt + 1) * 128],
                    src[tt][:, eb * 128:(eb + 1) * 128], self.idsb[:, :]),
                    reads=[stok, self.t_const, self.t_rB], writes=[pt], tag="tpfm")
            fl.op("V", lambda e, eb=eb, p=p: e.tensor_copy(
                dsts[eb][:, 0:WIN], p[:, 0:WIN]),
                reads=[pt, self.t_rB], writes=[dtok], tag="fmev")

    def emit_attention(self, w, qslot, slot_prev, slot_cur, first_glob):
        fl = self.fl
        QSUB, KSUB, WIN, DH, H = (self.QSUB, self.KSUB, self.WIN,
                                  self.DH, self.H)
        NPAIR, PACK2 = self.NPAIR, self.PACK2
        rB = self.t_rB
        nhp = 2 if PACK2 else 1
        kc_order = [QSUB - 1] + [k for k in range(KSUB) if k != QSUB - 1]
        for hp in range(NPAIR):
            pavs = [self.npg() for _ in range(nhp)]
            zps = [(self.PSC[2 + h01], self.t_psc[2 + h01])
                   for h01 in range(nhp)]
            for pos, kc in enumerate(kc_order):
                qlo = max(0, 128 * (kc - QSUB))
                qhi = min(WIN, 128 * kc + 128)
                wd = qhi - qlo
                sl = pos % 4
                for h01 in range(nhp):
                    hb = 64 * h01
                    wtb, wtok = self.wt[h01][sl], self.t_wt[h01][sl]
                    ps, pst = self.PSC[h01], self.t_psc[h01]
                    if kc < QSUB:
                        kt, ktok = self.kfm[slot_prev][hp], self.t_kfm[slot_prev]
                        kcol = kc * 128
                        mt, moff = self.tloTsb, wd - 128
                    else:
                        kt, ktok = self.kfm[slot_cur][hp], self.t_kfm[slot_cur]
                        kcol = (kc - QSUB) * 128
                        mt, moff = self.thiTsb, 0
                    fl.op("T", lambda e, kt=kt, kcol=kcol, hb=hb, hp=hp, qlo=qlo,
                          wd=wd, ps=ps: e.matmul(
                              ps[:, 0:wd], kt[hb:hb + 64, kcol:kcol + 128],
                              self.qfm[qslot][hp][hb:hb + 64, qlo:qlo + wd],
                              start=True, stop=False, tile_position=(hb, 0)),
                          reads=[ktok, self.t_qfm[qslot], rB], writes=[pst],
                          tag="sT")
                    if first_glob and kc < QSUB:
                        fl.op("T", lambda e, ps=ps, wd=wd: e.matmul(
                            ps[:, 0:wd], self.edgebf[:, 0:128],
                            self.oneswb[:, 0:wd],
                            start=False, stop=False),
                            reads=[self.t_const], writes=[pst], tag="eT")
                    fl.op("T", lambda e, ps=ps, mt=mt, moff=moff: e.matmul(
                        ps[:, moff:moff + 128], self.idsb[:, :], mt[:, :],
                        start=False, stop=True),
                        reads=[self.t_const], writes=[pst], tag="mT")
                    fl.op("S", lambda e, ps=ps, wtb=wtb, qlo=qlo, wd=wd:
                          e.activation(wtb[:, qlo:qlo + wd],
                                       ps[:, 0:wd], AF.Exp,
                                       bias=self.cvalsb[:, 5:6]),
                          reads=[pst, self.t_const, rB], writes=[wtok],
                          tag="wexp")
                for h01 in range(nhp):
                    hb = 64 * h01
                    h = 2 * hp + h01 if PACK2 else hp
                    wtb, wtok = self.wt[h01][sl], self.t_wt[h01][sl]
                    pav, pavt = pavs[h01]
                    zp, zpt = zps[h01]
                    if kc < QSUB:
                        vt, vtok = self.vtm[slot_prev][kc], self.t_vtm[slot_prev]
                    else:
                        vt, vtok = self.vtm[slot_cur][kc - QSUB], self.t_vtm[slot_cur]
                    fl.op("T", lambda e, vt=vt, h=h, hb=hb, wtb=wtb, pos=pos,
                          qlo=qlo, wd=wd, pav=pav: e.matmul(
                              pav[hb:hb + 64, qlo:qlo + wd],
                              vt[:, h * DH:(h + 1) * DH],
                              wtb[:, qlo:qlo + wd],
                              start=(pos == 0), stop=(pos == KSUB - 1),
                              tile_position=(0, hb)),
                          reads=[vtok, wtok, rB], writes=[pavt], tag="av")
                    fl.op("T", lambda e, wtb=wtb, pos=pos, qlo=qlo, wd=wd,
                          zp=zp: e.matmul(
                              zp[0:1, qlo:qlo + wd], self.onescb[:, :],
                              wtb[:, qlo:qlo + wd],
                              start=(pos == 0), stop=(pos == KSUB - 1)),
                          reads=[wtok, self.t_const, rB], writes=[zpt],
                          tag="zmm")
            for h01 in range(nhp):
                hb = 64 * h01
                pav, pavt = pavs[h01]
                zp, zpt = zps[h01]
                ps, pst = self.PSC[h01], self.t_psc[h01]
                rz = self.rowfl[0:1, h01 * WIN:(h01 + 1) * WIN]
                fl.op("S", lambda e, zp=zp, rz=rz: e.activation(
                    rz, zp[0:1, 0:WIN], AF.Ln),
                    reads=[zpt, self.t_rF], writes=[self.t_rowfl], tag="zln")
                fl.op("S", lambda e, rz=rz: e.activation(
                    rz, rz, AF.Exp, scale=self.cvalsb[0:1, 6:7]),
                    reads=[self.t_const, self.t_rF], writes=[self.t_rowfl],
                    tag="zrec")
                fl.op("T", lambda e, ps=ps, rz=rz: e.matmul(
                    ps[:, 0:WIN], self.onesw32[0:1, 0:128], rz,
                    start=True, stop=True),
                    reads=[self.t_rowfl, self.t_const, self.t_rF],
                    writes=[pst], tag="zB")
                fl.op("S", lambda e, ps=ps, h01=h01: e.copy(
                    self.zbb[h01][:, 0:WIN], ps[:, 0:WIN]),
                    reads=[pst, rB], writes=[self.t_zbb[h01]], tag="zev")
                fl.op("V", lambda e, pav=pav, hp=hp, hb=hb, h01=h01: e.tensor_mul(
                    self.ofm[0][hp][hb:hb + 64, 0:WIN],
                    pav[hb:hb + 64, 0:WIN], self.zbb[h01][hb:hb + 64, 0:WIN]),
                    reads=[pavt, self.t_zbb[h01], rB], writes=[self.t_ofm[0]],
                    tag="oev")

    def emit_wo(self, l, w):
        fl, DCH, ECH, WIN, XO = self.fl, self.DCH, self.ECH, self.WIN, self.XO
        for e2 in range(DCH):
            p, pt = self.npg()
            for o in range(ECH):
                fl.op("T", lambda e, e2=e2, o=o, w=w, p=p: e.matmul(
                    p[:, 0:WIN], self.wo_sb[o][:, e2 * 128:(e2 + 1) * 128],
                    self.ofm[0][o][:, 0:WIN],
                    start=(o == 0), stop=(o == ECH - 1)),
                    reads=[self.t_wo[o], self.t_ofm[0], self.t_rB],
                    writes=[pt], tag="womm")
            if l == 0:
                fl.op("V", lambda e, e2=e2, w=w, p=p: e.tensor_add(
                    self.x1[e2][:, XO + w * WIN:XO + (w + 1) * WIN],
                    self.xn[e2][:, self.LEXT + w * WIN:self.LEXT + (w + 1) * WIN],
                    p[:, 0:WIN]),
                    reads=[self.t_xn[e2], pt, self.t_rA],
                    writes=[self.t_x1[e2]], tag="xres")
            else:
                fl.op("V", lambda e, e2=e2, w=w, p=p: e.tensor_add(
                    self.x1[e2][:, XO + w * WIN:XO + (w + 1) * WIN],
                    self.x1[e2][:, XO + w * WIN:XO + (w + 1) * WIN],
                    p[:, 0:WIN]),
                    reads=[pt], writes=[self.t_x1[e2]], tag="xres")

    def emit_ag0(self, l):
        fl, DCH, NC_, SPC, XO = self.fl, self.DCH, self.NC, self.SPC, self.XO
        PI, NSEG = self.PI, self.NSEG
        nrk = NC_ * (PI + 1)
        for ch in range(DCH):
            fl.op("V", lambda e, ch=ch: e.tensor_copy(
                self.bstage[:, ch:ch + 1], self.x1[ch][:, XO + SPC - 1:XO + SPC]),
                reads=[self.t_x1[ch]], writes=[self.t_bstage], tag="bst")
        tr0 = [lambda e, l=l: e.dma_start(
            out=self.ag0_in[l][:, 0:DCH], in_=self.bstage[:, :]),
               lambda e, l=l: e.dma_start(
            out=self.ag0_in[l][:, DCH:2 * DCH], in_=self.fstage[:, :])]
        fl.dma("Q", tr0, reads=[self.t_bstage, self.t_fstage],
               writes=[self.t_ag0i[l]], tag="ag0i")
        fl.cc(lambda e, l=l, PI=PI: e.collective_compute(
            "AllGather", ALU.bypass, replica_groups=[list(range(NC_))],
            ins=[self.ag0_in[l].ap().opt()],
            outs=[self.ag0_out[l][PI * NC_:(PI + 1) * NC_, :, :].opt()]),
            reads=[self.t_ag0i[l]], writes=[self.t_ag0o[l]], tag="ag0")
        ag = self.ag0_out[l]
        fl.dma("Q", [lambda e, ag=ag, j=j: e.dma_start(
            out=self.bsel[:, j * DCH:(j + 1) * DCH], in_=ag[j, :, 0:DCH])
            for j in range(nrk)],
            reads=[self.t_ag0o[l]], writes=[self.t_bsel], tag="ag0rd")
        p, pt = self.npg()
        fl.op("T", lambda e, p=p, PI=PI, nrk=nrk: e.matmul(
            p[:, 0:nrk], self.onesw32[0:1, 0:128],
            self.onehotsb[0:1, PI * NSEG:PI * NSEG + nrk],
            start=True, stop=True),
            reads=[self.t_const], writes=[pt], tag="ohB")
        fl.op("V", lambda e, p=p, nrk=nrk: e.tensor_mul(
            self.bsel[:, 0:nrk * DCH].rearrange("p (j c) -> p j c", c=DCH),
            self.bsel[:, 0:nrk * DCH].rearrange("p (j c) -> p j c", c=DCH),
            bass.AP(p, 0, [[512, 128], [1, nrk], [0, DCH]])),
            reads=[pt], writes=[self.t_bsel], tag="ohM")
        fl.op("V", lambda e, nrk=nrk: e.reduce_sum(
            self.bstage[:, :],
            bass.AP(self.bsel, 0, [[NSEG * DCH, 128], [1, DCH], [DCH, nrk]]),
            axis=AX.X),
            reads=[self.t_bsel], writes=[self.t_bstage], tag="ohR")
        for ch in range(DCH):
            fl.op("V", lambda e, ch=ch: e.tensor_copy(
                self.x1[ch][:, 0:1], self.bstage[:, ch:ch + 1]),
                reads=[self.t_bstage], writes=[self.t_x1[ch]], tag="bwr")

    def emit_ffn(self, l):
        fl = self.fl
        DCH, SPC, TC, TCW, JS, D, XO = (self.DCH, self.SPC, self.TC, self.TCW,
                                        self.JS, self.D, self.XO)
        rB, rA = self.t_rB, self.t_rA

        def xsh(ch, lo, w):
            sh = 1 if ch >= DCH // 2 else 0
            return self.x1[ch][:, XO + lo - sh:XO + lo - sh + w]

        for t in range(TC):
            lo = t * TCW
            p0, pt0 = self.PG[0], self.t_pg[0]
            p1, pt1 = self.PG[1], self.t_pg[1]
            for ch in range(DCH):
                sl2 = ch % 2
                fl.op("V", lambda e, ch=ch, lo=lo, sl2=sl2: e.tensor_copy(
                    self.ysq_f[sl2][:, :], xsh(ch, lo, TCW)),
                    reads=[self.t_x1[ch], rB], writes=[self.t_ysqf[sl2]],
                    tag="lncst")
                fl.op("S", lambda e, sl2=sl2: e.square(
                    self.sqb2_f[sl2][:, :], self.ysq_f[sl2][:, :]),
                    reads=[self.t_ysqf[sl2], rB], writes=[self.t_sqb2f[sl2]],
                    tag="lnsq")
                fl.op("T", lambda e, p0=p0, ch=ch, sl2=sl2: e.matmul(
                    p0[0:1, 0:TCW], self.onescb[:, :], self.ysq_f[sl2][:, :],
                    start=(ch == 0), stop=(ch == DCH - 1)),
                    reads=[self.t_ysqf[sl2], self.t_const], writes=[pt0],
                    tag="lsum")
                fl.op("T", lambda e, p1=p1, ch=ch, sl2=sl2: e.matmul(
                    p1[0:1, 0:TCW], self.onescb[:, :], self.sqb2_f[sl2][:, :],
                    start=(ch == 0), stop=(ch == DCH - 1)),
                    reads=[self.t_sqb2f[sl2], self.t_const], writes=[pt1],
                    tag="lsq")
            fl.op("S", lambda e, lo=lo, p0=p0: e.copy(
                self.statr_sum[:, lo:lo + TCW], p0[0:1, 0:TCW]),
                reads=[pt0], writes=[self.t_statr], tag="lrow0")
            fl.op("S", lambda e, lo=lo, p1=p1: e.copy(
                self.statr_sq[:, lo:lo + TCW], p1[0:1, 0:TCW]),
                reads=[pt1], writes=[self.t_rsrow], tag="lrow1")
        fl.op("V", lambda e: e.tensor_scalar_mul(
            self.lmu[:, :], self.statr_sum[:, 0:SPC], 1.0 / D),
            reads=[rB], writes=[self.t_murow], tag="fmu")
        fl.op("V", lambda e: e.tensor_mul(self.lmr[:, :], self.lmu[:, :], self.lmu[:, :]),
              reads=[self.t_murow, rB], writes=[self.t_mrrow], tag="fmu2")
        fl.op("V", lambda e: e.scalar_tensor_tensor(
            self.lrs[:, :], self.statr_sq[:, 0:SPC], 1.0 / D, self.lmr[:, :],
            ALU.mult, ALU.subtract),
            reads=[self.t_mrrow, rB], writes=[self.t_rsrow], tag="fvar")
        fl.op("S", lambda e: e.activation(self.lrs[:, :], self.lrs[:, :], AF.Ln,
                                          bias=self.cvalsb[0:1, 0:1]),
              reads=[rB], writes=[self.t_rsrow], tag="flnv")
        fl.op("S", lambda e: e.activation(self.lrs[:, :], self.lrs[:, :], AF.Exp, scale=self.cvalsb[0:1, 2:3]),
              reads=[rB], writes=[self.t_rsrow], tag="frst")
        fl.op("V", lambda e: e.tensor_mul(self.lmr[:, :], self.lmu[:, :], self.lrs[:, :]),
              reads=[self.t_murow, self.t_rsrow, rB], writes=[self.t_mrrow], tag="fmurs")
        gcol = 5 if l == 0 else 8
        bcolv = 6 if l == 0 else 9
        for t in range(TC):
            lo = t * TCW
            pa, ta = self.npg()
            pb, tb = self.npg()
            fl.op("T", lambda e, lo=lo, pa=pa: e.matmul(
                pa[:, 0:TCW], self.onesw32[0:1, 0:128],
                self.lrs[0:1, lo:lo + TCW], start=True, stop=True),
                reads=[self.t_rsrow, self.t_const, rB], writes=[ta], tag="rgB")
            fl.op("T", lambda e, lo=lo, pb=pb: e.matmul(
                pb[:, 0:TCW], self.onesw32[0:1, 0:128],
                self.lmr[0:1, lo:lo + TCW], start=True, stop=True),
                reads=[self.t_mrrow, self.t_const, rB], writes=[tb], tag="cmb1")
            for ch in range(DCH):
                sg = self.stg1 if ch % 2 == 0 else self.stg2
                sgt = self.t_stg1 if ch % 2 == 0 else self.t_stg2
                fl.op("V", lambda e, ch=ch, lo=lo, pb=pb, sg=sg: e.tensor_sub(
                    sg[:, :], xsh(ch, lo, TCW), pb[:, 0:TCW]),
                    reads=[self.t_x1[ch], tb, rB], writes=[sgt], tag="xlr")
                fl.op("V", lambda e, pa=pa, sg=sg: e.tensor_mul(
                    sg[:, :], sg[:, :], pa[:, 0:TCW]),
                    reads=[ta, rB], writes=[sgt], tag="xlm")
                fl.op("V", lambda e, ch=ch, lo=lo, gcol=gcol, bcolv=bcolv, sg=sg:
                      e.tensor_scalar(
                          self.xln[ch][:, lo:lo + TCW], sg[:, :],
                          self.sv(ch, gcol), self.sv(ch, bcolv),
                          ALU.mult, ALU.add),
                      reads=[sgt, self.t_small, rA],
                      writes=[self.t_xln[ch]], tag="xlw")
        for ci in range(JS):
            sa, sg_ = (2 * ci) % 6, (2 * ci + 1) % 6
            fl.dma("G", [
                lambda e, ci=ci, sa=sa, l=l: e.dma_start(
                    out=self.w1_sb[sa][:, :], in_=self.w1a_d[l, ci, :, :]),
                lambda e, ci=ci, sg_=sg_, l=l: e.dma_start(
                    out=self.w1_sb[sg_][:, :], in_=self.w1g_d[l, ci, :, :]),
            ], reads=[rB], writes=[self.t_w1[sa], self.t_w1[sg_]], tag="w1ld")
            cb = ci % 2
            sa_, sg2 = (2 * ci) % 6, (2 * ci + 1) % 6
            ba = self.b1sb[:, (2 * l) * JS + ci:(2 * l) * JS + ci + 1]
            bg = self.b1sb[:, (2 * l + 1) * JS + ci:(2 * l + 1) * JS + ci + 1]
            hb_ = self.hgst[cb][0]
            hbt = self.t_hgst[cb][0]
            geb = self.hgst[cb][1]
            gebt = self.t_hgst[cb][1]
            for t in range(TC):
                lo = t * TCW
                pa_, pat = self.npgF()
                pg2, pgt = self.npgF()
                for d in range(DCH):
                    fl.op("T", lambda e, d=d, lo=lo, p=pa_, slot=sa_: e.matmul(
                        p[:, 0:TCW], self.w1_sb[slot][:, d * 128:(d + 1) * 128],
                        self.xln[d][:, lo:lo + TCW],
                        start=(d == 0), stop=(d == DCH - 1)),
                        reads=[self.t_w1[sa_], self.t_xln[d], rA, rB],
                        writes=[pat], tag="w1mm")
                for d in range(DCH):
                    fl.op("T", lambda e, d=d, lo=lo, p=pg2, slot=sg2: e.matmul(
                        p[:, 0:TCW], self.w1_sb[slot][:, d * 128:(d + 1) * 128],
                        self.xln[d][:, lo:lo + TCW],
                        start=(d == 0), stop=(d == DCH - 1)),
                        reads=[self.t_w1[sg2], self.t_xln[d], rA, rB],
                        writes=[pgt], tag="w1mm")
                fl.op("S", lambda e, p=pg2, lo=lo, geb=geb, bg=bg:
                      e.activation(geb[:, lo:lo + TCW], p[:, 0:TCW],
                                   AF.Gelu, bias=bg),
                      reads=[pgt, self.t_b1, rB], writes=[gebt], tag="gev")
                fl.op("V", lambda e, p=pa_, lo=lo, hb_=hb_, geb=geb, ba=ba:
                      e.scalar_tensor_tensor(
                          hb_[:, lo:lo + TCW], p[:, 0:TCW], ba,
                          geb[:, lo:lo + TCW], ALU.add, ALU.mult),
                      reads=[pat, gebt, self.t_b1, rB], writes=[hbt],
                      tag="hgstt")
            fl.dma("Q", [lambda e, ci=ci, hb_=hb_: e.dma_start(
                out=self.hg_dram[ci, :, :], in_=hb_[:, :])],
                reads=[hbt], writes=[self.t_hgd[ci]], tag="hgsp")
        JSH = self.JSH
        for fb in range(2):
            f0 = fb * JSH
            f1 = min(JS, f0 + JSH)
            if f0 >= f1:
                continue
            fl.dma("G", [lambda e, f=f, l=l, f0=f0: e.dma_start(
                out=self.w2t[f - f0][:, :], in_=self.w2_d[l, f, :, :])
                for f in range(f0, f1)],
                reads=[rB], writes=[self.t_w2t[f - f0] for f in range(f0, f1)],
                tag="w2ld")
            for t in range(TC):
                lo = t * TCW
                if (fb * TC + t) % 2 == 0:
                    hgb = self.hgw
                    hgbt = self.t_hgw
                else:
                    cps = max(1, (self.DCH * 128) // TCW)
                    hgb = [self.w1_sb[i // cps][:, (i % cps) * TCW:
                                                (i % cps + 1) * TCW]
                           for i in range(self.JSH)]
                    hgbt = [self.t_w1[i // cps] for i in range(self.JSH)]
                fl.dma("Q", [lambda e, f=f, lo=lo, f0=f0, hgb=hgb: e.dma_start(
                    out=hgb[f - f0][:, :], in_=self.hg_dram[f, :, lo:lo + TCW])
                    for f in range(f0, f1)],
                    reads=[self.t_hgd[f] for f in range(f0, f1)] + [rB],
                    writes=list({id(tk): tk for tk in
                                 [hgbt[f - f0] for f in range(f0, f1)]}.values()),
                    tag="hgld")
                for e2 in range(DCH):
                    p, pt = self.npgF()
                    for f in range(f0, f1):
                        fl.op("T", lambda e, f=f, e2=e2, p=p, f0=f0, f1=f1,
                              hgb=hgb: e.matmul(
                                  p[:, 0:TCW],
                                  self.w2t[f - f0][:, e2 * 128:(e2 + 1) * 128],
                                  hgb[f - f0][:, :],
                                  start=(f == f0), stop=(f == f1 - 1)),
                              reads=[self.t_w2t[f - f0], hgbt[f - f0], rB],
                              writes=[pt], tag="w2mm")
                    if fb == 0 and JS > JSH:
                        fl.op("V", lambda e, e2=e2, lo=lo, p=p: e.tensor_add(
                            self.x1[e2][:, XO + lo:XO + lo + TCW],
                            self.x1[e2][:, XO + lo:XO + lo + TCW], p[:, 0:TCW]),
                            reads=[pt], writes=[self.t_x1[e2]], tag="w2res0")
                    else:
                        b2c = self.sv(e2, 7 if l == 0 else 10)
                        fl.op("V", lambda e, e2=e2, lo=lo, p=p, b2c=b2c:
                              e.scalar_tensor_tensor(
                                  self.x1[e2][:, XO + lo:XO + lo + TCW],
                                  p[:, 0:TCW], b2c,
                                  self.x1[e2][:, XO + lo:XO + lo + TCW],
                                  ALU.add, ALU.add),
                              reads=[pt, self.t_small], writes=[self.t_x1[e2]],
                              tag="w2res")

    def phase_layer(self, l):
        fl, NW, QSUB = self.fl, self.NW, self.QSUB
        self.load_layer_weights(l)
        if l == 1:
            for ch in range(self.DCH):
                fl.op("V", lambda e, ch=ch: e.tensor_copy(
                    self.x1b[ch][:, self.LEXT:self.LEXT + self.SPC],
                    self.x1[ch][:, self.XO:self.XO + self.SPC]),
                    reads=[self.t_x1[ch], self.t_rA], writes=[self.t_x1b[ch]],
                    tag="x1bc")
        self.switch_region(self.t_rB, "attnL%d" % l)
        self.switch_region(self.t_rF, "attnF%d" % l)
        self.emit_eff(l)
        for w in range(NW):
            for j in range(QSUB):
                self.emit_qkv_tile(l, QSUB + w * QSUB + j, w * QSUB + j)
            self.emit_fm_transpose("k", (w + 1) % 2, 0)
            self.emit_fm_transpose("q", 0, 0)
            if w == 0:
                for j in range(QSUB):
                    self.emit_qkv_tile(l, j, None)
                self.emit_fm_transpose("k", 0, 0)
            self.emit_attention(w, 0, w % 2, (w + 1) % 2,
                                w == 0 and self.PI == 0)
            self.emit_wo(l, w)
        self.emit_ag0(l)
        self.emit_dbg(2 * l)
        self.switch_region(self.t_rB, "ffnL%d" % l)
        self.emit_ffn(l)
        self.emit_dbg(2 * l + 1)

    def emit_dbg(self, i):
        if not getattr(self, "DEBUG", False):
            return
        PI = self.PI
        for ch in range(self.DCH):
            self.fl.dma("G", [lambda e, ch=ch, i=i, PI=PI: e.dma_start(
                out=self.dbg_dram[i][PI, ch, :, :], in_=self.x1[ch][:, :])],
                reads=[self.t_x1[ch]], tag="dbg")

    def phase_ag1_and_s(self):
        fl = self.fl
        DCH, NC_, WIN, SPC, XO, TC, TCW = (self.DCH, self.NC, self.WIN, self.SPC,
                                           self.XO, self.TC, self.TCW)
        rB, rA = self.t_rB, self.t_rA
        AG1W = self.AG1W
        PI, NSEG = self.PI, self.NSEG
        nrk = NC_ * (PI + 1)
        co = PI * 2 * NSEG
        tr = []
        for ch in range(DCH):
            c0 = ch * (WIN + 1)
            tr.append(lambda e, ch=ch, c0=c0: e.dma_start(
                out=self.ag1_in[:, c0:c0 + WIN + 1],
                in_=self.x1[ch][:, XO + SPC - WIN - 1:XO + SPC]))
        fl.dma("G", tr, reads=[self.t_x1[ch] for ch in range(DCH)],
               writes=[self.t_ag1i], tag="ag1i")
        fl.cc(lambda e, PI=PI: e.collective_compute(
            "AllGather", ALU.bypass, replica_groups=[list(range(NC_))],
            ins=[self.ag1_in.ap().opt()],
            outs=[self.ag1_out[PI * NC_:(PI + 1) * NC_, :, :].opt()]),
            reads=[self.t_ag1i], writes=[self.t_ag1o], tag="ag1")

        self.rv_prev = [None]

        def ldprev(e, PI=PI):
            self.rv_prev[0] = e.value_load(self.prevsb[0:1, PI:PI + 1])
            return e.engine_nop()
        fl.op("G", ldprev, reads=[self.t_const], writes=[self.t_fin], tag="ldprev")

        tr = []
        for ch in range(DCH):
            c0 = ch * (WIN + 1)

            def t(e, ch=ch, c0=c0):
                src = self.ag1_out[bass.ds(self.rv_prev[0], 1), :, c0:c0 + WIN + 1]
                return e.dma_start(
                    out=self.x1b[ch][:, self.LEXT - WIN - 1:self.LEXT], in_=src)
            tr.append(t)
        fl.dma("G", tr, reads=[self.t_ag1o, self.t_fin, self.t_rA],
               writes=[self.t_x1b[ch] for ch in range(DCH)], tag="halo")
        for ch in range(DCH):
            fl.op("V", lambda e, ch=ch, PI=PI: e.tensor_tensor(
                self.x1b[ch][:, self.LEXT - 1:self.LEXT],
                self.x1b[ch][:, self.LEXT - 1:self.LEXT],
                self.bcolb[:, PI:PI + 1], ALU.mult),
                reads=[self.t_x1b[ch], self.t_const],
                writes=[self.t_x1b[ch]], tag="bgate")
        fl.dma("Q", [lambda e, j=j, PI=PI: e.dma_start(
            out=self.fincol[:, (PI * NC_ + j) * DCH:(PI * NC_ + j + 1) * DCH],
            in_=self.ag0_out[0][PI * NC_ + j, :, DCH:2 * DCH])
            for j in range(NC_)],
            reads=[self.t_ag0o[0]], writes=[self.t_fincol], tag="fread")

        self.switch_region(self.t_rB, "sasm")
        self.switch_region(self.t_rF, "sasmF")
        fl.dma("Q", [lambda e: e.dma_start(out=self.iotasb[:, :],
                                           in_=self.i_iota[:, :])],
               reads=[self.t_rF], writes=[self.t_iota], tag="iotald")
        fl.dma("Q", [lambda e, ch=ch: e.dma_start(
            out=self.hres[ch][:, :], in_=self.h_dram[ch, :, :]) for ch in range(DCH)],
            reads=[self.t_hd[ch] for ch in range(DCH)] + [rB],
            writes=[self.t_hres[ch] for ch in range(DCH)], tag="hre")


        pm, pmt = self.npg()
        fl.op("T", lambda e, pm=pm, co=co, nrk=nrk: e.matmul(
            pm[:, 0:nrk], self.onesw32[0:1, 0:128],
            self.carrysb[0:1, co + NSEG:co + NSEG + nrk],
            start=True, stop=True),
            reads=[self.t_const], writes=[pmt], tag="mgB")
        pp, ppt = self.npg()
        fl.op("T", lambda e, pp=pp, co=co, nrk=nrk: e.matmul(
            pp[:, 0:nrk], self.onesw32[0:1, 0:128],
            self.carrysb[0:1, co:co + nrk],
            start=True, stop=True),
            reads=[self.t_const], writes=[ppt], tag="cpB")
        for ch in range(DCH):
            fl.op("S", lambda e, ch=ch, pp=pp, nrk=nrk: e.activation(
                self.cwt[:, 0:nrk], pp[:, 0:nrk], AF.Exp, scale=self.sv(ch, 2)),
                reads=[ppt, self.t_small], writes=[self.t_cwt], tag="cwexp")
            fl.op("V", lambda e, pm=pm, nrk=nrk: e.tensor_mul(
                self.cwt[:, 0:nrk], self.cwt[:, 0:nrk], pm[:, 0:nrk]),
                reads=[pmt], writes=[self.t_cwt], tag="cwm")
            fl.op("V", lambda e, ch=ch, nrk=nrk: e.tensor_mul(
                self.cwt[:, 0:nrk], self.cwt[:, 0:nrk],
                bass.AP(self.fincol, ch, [[NSEG * DCH, 128], [DCH, nrk]])),
                reads=[self.t_fincol], writes=[self.t_cwt], tag="cwf")
            fl.op("V", lambda e, ch=ch, nrk=nrk: e.reduce_sum(
                self.hincol[:, ch:ch + 1], self.cwt[:, 0:nrk], axis=AX.X),
                reads=[self.t_cwt], writes=[self.t_hincol], tag="hin")
        for t in range(TC):
            lo = t * TCW
            pw, pwt = self.npg()
            fl.op("T", lambda e, lo=lo, pw=pw: e.matmul(
                pw[:, 0:TCW], self.onesw32[0:1, 0:128],
                self.iotasb[:, lo:lo + TCW], start=True, stop=True),
                reads=[self.t_const, self.t_iota, self.t_rF],
                writes=[pwt], tag="iotB")
            for ch in range(DCH):
                fl.op("S", lambda e, ch=ch, pw=pw: e.activation(
                    self.powt[:, :], pw[:, 0:TCW], AF.Exp, scale=self.sv(ch, 2)),
                    reads=[pwt, self.t_small, self.t_rF],
                    writes=[self.t_powt], tag="powt")
                fl.op("V", lambda e, ch=ch, lo=lo: e.scalar_tensor_tensor(
                    self.hres[ch][:, lo:lo + TCW], self.powt[:, :],
                    self.hincol[:, ch:ch + 1], self.hres[ch][:, lo:lo + TCW],
                    ALU.mult, ALU.add),
                    reads=[self.t_powt, self.t_hincol, rB, self.t_rF],
                    writes=[self.t_hres[ch]], tag="hcorr")
        fl.dma("G", [lambda e, ch=ch: e.dma_start(
            out=self.owf[ch][:, :], in_=self.outw_d[ch, :, :]) for ch in range(DCH)],
            reads=[rB], writes=[self.t_owf[ch] for ch in range(DCH)], tag="owld")
        for t in range(TC):
            lo = t * TCW
            prms, prmst = self.PG[2], self.t_pg[2]
            for e2 in range(DCH):
                p, pt = self.npg2()
                for ch in range(DCH):
                    fl.op("T", lambda e, ch=ch, e2=e2, lo=lo, p=p: e.matmul(
                        p[:, 0:TCW], self.owf[ch][:, e2 * 128:(e2 + 1) * 128],
                        self.hres[ch][:, lo:lo + TCW],
                        start=(ch == 0), stop=(ch == DCH - 1)),
                        reads=[self.t_owf[ch], self.t_hres[ch], rB],
                        writes=[pt], tag="opmm")
                xsl = e2 % 2
                fl.dma("Q", [lambda e, e2=e2, lo=lo, xsl=xsl: e.dma_start(
                    out=self.xnre[xsl][:, :],
                    in_=self.xn_dram[e2, :, self.LEXT + lo:self.LEXT + lo + TCW])],
                    reads=[self.t_xnd[e2], rB], writes=[self.t_xnre[xsl]],
                    tag="xnre")
                sg = self.stg1 if e2 % 2 == 0 else self.stg2
                sgt = self.t_stg1 if e2 % 2 == 0 else self.t_stg2
                fl.op("V", lambda e, p=p, e2=e2, sg=sg, xsl=xsl:
                      e.scalar_tensor_tensor(
                          sg[:, :], p[:, 0:TCW], self.sv(e2, 3),
                          self.xnre[xsl][:, :], ALU.add, ALU.add),
                      reads=[pt, self.t_small, self.t_xnre[xsl]],
                      writes=[sgt], tag="yev")
                fl.op("S", lambda e, sg=sg: e.square(self.ysq_s[:, :], sg[:, :]),
                      reads=[sgt, rB], writes=[self.t_ysq], tag="ysq")
                fl.op("T", lambda e, e2=e2, prms=prms: e.matmul(
                    prms[0:1, 0:TCW], self.onescb[:, :], self.ysq_s[:, :],
                    start=(e2 == 0), stop=(e2 == DCH - 1)),
                    reads=[self.t_ysq, self.t_const], writes=[prmst], tag="rmsmm")
                fl.dma("Q", [lambda e, e2=e2, lo=lo, sg=sg: e.dma_start(
                    out=self.y_dram[e2, :, lo:lo + TCW], in_=sg[:, :])],
                    reads=[sgt], writes=[self.t_yd[e2]], tag="ysp")
            fl.op("S", lambda e, lo=lo, prms=prms: e.copy(
                self.statr_sum[:, lo:lo + TCW], prms[0:1, 0:TCW]),
                reads=[prmst], writes=[self.t_statr], tag="rmsrow")
        fl.op("S", lambda e: e.activation(
            self.rmsrow[:, :], self.statr_sum[:, 0:SPC], AF.Ln,
            scale=self.cvalsb[0:1, 3:4], bias=self.cvalsb[0:1, 1:2]),
            reads=[self.t_statr], writes=[self.t_rmsrow], tag="rln")
        fl.op("S", lambda e: e.activation(self.rmsrow[:, :], self.rmsrow[:, :], AF.Exp,
                                          scale=self.cvalsb[0:1, 2:3]),
              writes=[self.t_rmsrow], tag="rex")

    def phase_final(self):
        fl = self.fl
        DCH, SPC, TC, TCW, XO = self.DCH, self.SPC, self.TC, self.TCW, self.XO
        PI = self.PI
        rB = self.t_rF
        self.switch_region(rB, "final")
        outtoks = []
        for ch in range(DCH):
            fl.dma("Q", [
                lambda e, ch=ch: e.dma_start(out=self.yre[:, :],
                                             in_=self.y_dram[ch, :, :]),
                lambda e, ch=ch, PI=PI: e.dma_start(
                    out=self.xre[:, :],
                    in_=self.xT[PI, 128 * ch:128 * (ch + 1),
                                self.LEXT:self.LEXT + SPC]),
            ], reads=[self.t_yd[ch], rB], writes=[self.t_yre, self.t_xre],
                tag="finld")
            for t in range(TC):
                lo = t * TCW
                fl.op("S", lambda e, ch=ch, lo=lo: e.activation(
                    self.stg1[:, :], self.x1[ch][:, XO + lo:XO + lo + TCW],
                    AF.Silu),
                    reads=[self.t_x1[ch]], writes=[self.t_stg1], tag="silu")
                pr, prt = self.npg()
                fl.op("T", lambda e, lo=lo, pr=pr: e.matmul(
                    pr[:, 0:TCW], self.onesw32[32:33, 0:128],
                    self.rmsrow[0:1, lo:lo + TCW], start=True, stop=True),
                    reads=[self.t_rmsrow, self.t_const], writes=[prt], tag="rmsB")
                fl.op("V", lambda e, ch=ch, lo=lo, pr=pr: e.scalar_tensor_tensor(
                    self.stg2[:, :], self.yre[:, lo:lo + TCW], self.sv(ch, 4),
                    pr[:, 0:TCW], ALU.mult, ALU.mult),
                    reads=[self.t_yre, prt, self.t_small, rB],
                    writes=[self.t_stg2], tag="smul")
                fl.op("V", lambda e: e.tensor_mul(
                    self.stg1[:, :], self.stg1[:, :], self.stg2[:, :]),
                    reads=[self.t_stg2], writes=[self.t_stg1], tag="sg")
                fl.op("V", lambda e, lo=lo: e.tensor_add(
                    self.obuf[:, lo:lo + TCW], self.stg1[:, :],
                    self.xre[:, lo:lo + TCW]),
                    reads=[self.t_stg1, self.t_xre, rB], writes=[self.t_obuf],
                    tag="oadd")
            tok = fl.dma("Q", [lambda e, ch=ch, PI=PI: e.dma_start(
                out=self.oT[128 * ch:128 * (ch + 1), PI * SPC:(PI + 1) * SPC],
                in_=self.obuf[:, :])],
                reads=[self.t_obuf, rB], tag="ost")
            outtoks.append(tok)
        fin = Tok()
        for tk in outtoks:
            fin.readers.append(tk) if False else None
        class _W:
            pass
        dummy = Tok()
        dummy.w = outtoks[-1]
        dummy.readers = list(outtoks[:-1])
        fl.op("V", lambda e: e.engine_nop(), writes=[dummy], tag="finwait")


def _prep_core_inputs(c, inputs, core):
    D, SPC, WIN, LEXT, REXT, XE = (c["D"], c["SPC"], c["WIN"], c["LEXT"],
                                   c["REXT"], c["XE"])
    NC_, JS, DCH, DH = c["NC"], c["JS"], c["DCH"], c["DH"]
    NP, NSEG = c["PASSES"], c["NSEG"]
    xfull = inputs["xTfull"]
    xe = np.zeros((NP, D, XE), np.float32)
    tabq_l, tabk_l = [], []
    edge = np.zeros((1, WIN), np.float32)
    carry = np.zeros((1, NP * 2 * NSEG), np.float32)
    onehot = np.zeros((1, NP * NSEG), np.float32)
    prevrk = np.zeros((1, NP), np.int32)
    bcol = np.ones((128, NP), np.float32)
    for p in range(NP):
        seg = p * NC_ + core
        T0 = seg * SPC
        lo = T0 - LEXT
        hi = T0 + SPC + REXT
        slo, shi = max(0, lo), min(c["SEQ"], hi)
        xe[p, :, slo - lo:shi - lo] = xfull[:, slo:shi]
        tq, tk = host_tables(c, seg)
        tabq_l.append(tq)
        tabk_l.append(tk)
        if seg == 0:
            edge[0, :] = NEG
            bcol[:, p] = 0.0
        co = p * 2 * NSEG
        for j in range(NSEG):
            if j < seg:
                carry[0, co + j] = SPC * (seg - 1 - j)
                carry[0, co + NSEG + j] = 1.0
        if seg > 0:
            onehot[0, p * NSEG + seg - 1] = 1.0
        prevrk[0, p] = (seg - 1) if seg > 0 else 0
    d = dict(inputs["shared"])
    d.update(dict(xT=xe, tabq=np.stack(tabq_l), tabk=np.stack(tabk_l),
                  edge_neg=edge, carry=carry,
                  onehot=onehot, prevrk=prevrk, bcol=bcol))
    return d


def prep_shared(c, L):
    D, JS, DCH, DH, NC_ = c["D"], c["JS"], c["DCH"], c["DH"], c["NC"]
    HID = c["HID"]
    wqkv = np.ascontiguousarray(L["ffn_wqkv"], np.float32)
    wo = host_tile_rows(np.ascontiguousarray(L["ffn_wo"], np.float32),
                        c["INNER"] // 128)
    w1 = np.ascontiguousarray(L["ffn_w1"], np.float32)
    w1a = host_tile_w(w1[:, :, :HID], JS)
    w1g = host_tile_w(w1[:, :, HID:], JS)
    w2 = host_tile_rows(np.ascontiguousarray(L["ffn_w2"], np.float32), JS)
    outw_t = np.ascontiguousarray(L["attn_out_w"], np.float32).T
    outw = np.ascontiguousarray(
        outw_t.reshape(DCH, 128, D)).astype(np.float32)
    gatew = host_pack_taps(L["attn_gate_w"].reshape(D, -1).astype(np.float32))
    candw = host_pack_taps(L["attn_cand_w"].reshape(D, -1).astype(np.float32))
    z = np.zeros(D, np.float32)
    dv = host_pack_dvec([
        L["attn_gate_b"].astype(np.float32), L["attn_cand_b"].astype(np.float32),
        L["attn_log_decay"].astype(np.float32), L["attn_out_b"].astype(np.float32),
        L["attn_norm_w"].astype(np.float32),
        L["ffn_ln_g"][0].astype(np.float32), L["ffn_ln_b"][0].astype(np.float32),
        L["ffn_b2"][0].astype(np.float32),
        L["ffn_ln_g"][1].astype(np.float32), L["ffn_ln_b"][1].astype(np.float32),
        L["ffn_b2"][1].astype(np.float32)])
    b1 = np.asarray(L["ffn_b1"], np.float32)
    JP = JS * 128
    b1p = np.zeros((128, 4 * JS), np.float32)
    for l in range(2):
        for grp in range(2):
            seg = b1[l, :HID] if grp == 0 else b1[l, HID:]
            segp = np.zeros(JP, np.float32)
            segp[:HID] = seg
            for ci in range(JS):
                b1p[:, (2 * l + grp) * JS + ci] = segp[128 * ci:128 * (ci + 1)]
    qks = np.concatenate([
        L["ffn_q_scale"][0:1], L["ffn_k_scale"][0:1],
        L["ffn_q_scale"][1:2], L["ffn_k_scale"][1:2]],
        axis=0).astype(np.float32).reshape(1, 4 * DH)
    rowv = np.stack([L["ffn_ln_g"][0], L["ffn_ln_b"][0],
                     L["ffn_ln_g"][1], L["ffn_ln_b"][1]]
                    ).astype(np.float32).reshape(1, 4 * D)
    return dict(wqkv=wqkv, wo_t=wo, w1a_t=w1a, w1g_t=w1g, w2_t=w2,
                outw_t=outw, gatew=gatew, candw=candw, dvecs=dv, b1pack=b1p,
                qkscale=qks, rowvec=rowv)


_NC_CACHE = {}


def _get_nc(key, c):
    if key not in _NC_CACHE:
        _NC_CACHE[key] = MK(c).build()
    return _NC_CACHE[key]


def run(c, inputs_by_name):
    L = {k[3:]: v for k, v in inputs_by_name.items() if k.startswith("L0_")}
    x = np.asarray(L["attn_x"], np.float32)[0]
    xTfull = np.ascontiguousarray(x.T)
    shared = prep_shared(c, L)
    nc = _get_nc(("mk", c["NC"], c["D"], c["SPC"]), c)
    in_maps = []
    base = dict(shared=shared, xTfull=xTfull)
    for core in range(c["NC"]):
        in_maps.append(_prep_core_inputs(c, base, core))
    res = run_bass_kernel_spmd(nc, in_maps, core_ids=list(range(c["NC"])))
    outs = [np.asarray(r["oT"], np.float32) for r in res.results]
    D, SPC, NC_, NP = c["D"], c["SPC"], c["NC"], c["PASSES"]
    full = np.empty((D, c["SEQ"]), np.float32)
    for core in range(NC_):
        for p in range(NP):
            seg = p * NC_ + core
            full[:, seg * SPC:(seg + 1) * SPC] = outs[core][:, p * SPC:(p + 1) * SPC]
    return np.ascontiguousarray(full.T)[None]


def kernel(**inputs):
    c = cfg_full()
    return run(c, inputs)
